# version 11
# baseline (speedup 1.0000x reference)
import numpy as np
import concourse.bass as bass
import concourse.mybir as mybir
from concourse.bass_utils import run_bass_kernel_spmd

F32 = mybir.dt.float32
BF16 = mybir.dt.bfloat16
I16 = mybir.dt.int16
U8 = mybir.dt.uint8
AF = mybir.ActivationFunctionType
ALU = mybir.AluOpType
AX = mybir.AxisListType

B_, S, D = 4, 8192, 2048
NMLP = 16384
TLOC = 4096
HC = 16
NSTRIP = 64
GELU_A = 1.5957691216056908
GELU_B = 0.044715

DBG = []


def _perm_h():
    p = []
    for m in range(8):
        p += [256 * m + 2 * i for i in range(128)]
        p += [256 * m + 2 * i + 1 for i in range(128)]
    return np.array(p, dtype=np.int64)


class Emit:
    def __init__(self, nc):
        self.nc = nc
        self.streams = {k: [] for k in ("sync", "gpsimd", "scalar", "vector", "tensor")}
        self.sems = {}
        self.cnt = {}

    def sem(self, name):
        h = self.nc.semaphore(name).__enter__()
        self.sems[name] = h
        self.cnt[name] = 0

    def add(self, eng, fn):
        self.streams[eng].append(fn)

    def dma(self, eng, out_ap, in_ap, sem_name):
        h = self.sems[sem_name]
        self.cnt[sem_name] += 16
        tgt = self.cnt[sem_name]

        def _f(e, out_ap=out_ap, in_ap=in_ap, h=h):
            e.dma_start(out=out_ap, in_=in_ap).then_inc(h, 16)
        self.add(eng, _f)
        return tgt

    def op(self, eng, builder, inc=None):
        tgt, h = None, None
        if inc is not None:
            self.cnt[inc] += 1
            tgt = self.cnt[inc]
            h = self.sems[inc]

        def _f(e, builder=builder, h=h):
            ins = builder(e)
            if h is not None:
                ins.then_inc(h, 1)
        self.add(eng, _f)
        return tgt

    def wait(self, eng, sem_name, val):
        if not val:
            return
        h = self.sems[sem_name]

        def _f(e, h=h, val=val):
            e.wait_ge(h, val)
        self.add(eng, _f)

    def now(self, sem_name):
        return self.cnt[sem_name]


def build_program(dbg=(), last_phase='E'):
    nc = bass.Bass("TRN2", target_bir_lowering=False)
    em = Emit(nc)
    DMA_SEMS = ["LXT0", "LXT1", "LXR", "LQS", "LKC0", "LKC1", "LAR", "LXC",
                "LAT0", "LAT1", "LXH", "LWG0", "LWG1", "LWD", "LIC0", "LIC1", "LXA",
                "GCAST", "GDEC0", "GDEC1", "GDWR0", "GDWR1", "GXW0", "GXW1",
                "GQW0", "GQW1", "GAW", "GC2W0", "GC2W1", "GAFW0", "GAFW1",
                "GIW0", "GIW1", "GIW2", "GIW3", "GOW0", "GOW1", "GDBG", "GAGR"]
    for s_ in ["SPE", "SDV", "SACT"] + DMA_SEMS:
        em.sem(s_)
    em.dma_sems = DMA_SEMS

    x_tm = nc.dram_tensor("x_tm", [S, D], F32, kind="ExternalInput").ap()
    wq_i = nc.dram_tensor("wq", [D, D], F32, kind="ExternalInput").ap()
    wk_i = nc.dram_tensor("wk", [D, D], F32, kind="ExternalInput").ap()
    wv_i = nc.dram_tensor("wv", [D, D], F32, kind="ExternalInput").ap()
    wo_i = nc.dram_tensor("wo", [D, D], F32, kind="ExternalInput").ap()
    pg16 = nc.dram_tensor("pg16", [1024, 8192], I16, kind="ExternalInput").ap()
    pu16 = nc.dram_tensor("pu16", [1024, 8192], I16, kind="ExternalInput").ap()
    pd16 = nc.dram_tensor("pd16", [8192, 1024], I16, kind="ExternalInput").ap()
    sg_s = [nc.dram_tensor(f"sg{p}", [64, 8192], F32, kind="ExternalInput").ap() for p in range(2)]
    su_s = [nc.dram_tensor(f"su{p}", [64, 8192], F32, kind="ExternalInput").ap() for p in range(2)]
    sd_s = [nc.dram_tensor(f"sd{p}", [512, 1024], F32, kind="ExternalInput").ap() for p in range(2)]
    idb_i = nc.dram_tensor("idb", [128, 128], BF16, kind="ExternalInput").ap()
    out_loc = nc.dram_tensor("out_loc", [TLOC, D], F32, kind="ExternalOutput").ap()

    sc = {
        "xn_sc": nc.dram_tensor("xn_sc", [HC, 128, TLOC], BF16).ap(),
        "q_sc": nc.dram_tensor("q_sc", [S, D], BF16).ap(),
        "k_sc": nc.dram_tensor("k_sc", [S, D], BF16).ap(),
        "a_sc": nc.dram_tensor("a_sc", [D, D], BF16).ap(),
        "attnT_sc": nc.dram_tensor("attnT_sc", [HC, 128, TLOC], BF16).ap(),
        "afin_sc": nc.dram_tensor("afin_sc", [TLOC, D], F32).ap(),
        "wg_dec": nc.dram_tensor("wg_dec", [HC, 128, NMLP], BF16).ap(),
        "wu_dec": nc.dram_tensor("wu_dec", [HC, 128, NMLP], BF16).ap(),
        "wd_dec": nc.dram_tensor("wd_dec", [NMLP, D], BF16).ap(),
        "inter_sc": nc.dram_tensor("inter_sc", [128, 128, TLOC], BF16).ap(),
    }
    dbg_outs = {}
    for name in dbg:
        t = sc[name].tensor
        dbg_outs[name] = nc.dram_tensor("dbg_" + name, list(t.shape), t.dtype,
                                        kind="ExternalOutput").ap()

    big0 = nc.alloc_sbuf_tensor("big0", [128, 65536], U8).ap()
    big1 = nc.alloc_sbuf_tensor("big1", [128, 65536], U8).ap()
    aren = nc.alloc_sbuf_tensor("aren", [128, 32768], U8).ap()

    def carve(base, off, nbytes, dtype, shape):
        ap = base[:, off:off + nbytes].bitcast(dtype)
        if len(shape) == 1:
            return ap
        spec = "p (" + " ".join(f"d{i}" for i in range(len(shape))) + ") -> p " + \
               " ".join(f"d{i}" for i in range(len(shape)))
        return ap.rearrange(spec, **{f"d{i}": shape[i] for i in range(len(shape))})

    xrow_sb = nc.alloc_sbuf_tensor("xrow", [128, D], F32).ap()
    sqs_sb = nc.alloc_sbuf_tensor("sqs", [128, D], F32).ap()
    stat_sb = nc.alloc_sbuf_tensor("stat", [128, 8], F32).ap()
    bst_sb = nc.alloc_sbuf_tensor("bst", [128, 8], F32).ap()
    ones_sb = nc.alloc_sbuf_tensor("ones1", [1, 128], F32).ap()
    eps_sb = nc.alloc_sbuf_tensor("eps1", [128, 1], F32).ap()
    idb_sb = nc.alloc_sbuf_tensor("idb_sb", [128, 128], BF16).ap()
    d_in = nc.alloc_sbuf_tensor("d_in", [128, 2, 256], I16).ap()
    d_t = nc.alloc_sbuf_tensor("d_t", [128, 4, 256], I16).ap()
    d_mag = nc.alloc_sbuf_tensor("d_mag", [128, 4, 256], I16).ap()
    d_sb2 = nc.alloc_sbuf_tensor("d_sb2", [128, 4, 256], I16).ap()
    d_t3 = nc.alloc_sbuf_tensor("d_t3", [128, 4, 256], BF16).ap()
    d_out = nc.alloc_sbuf_tensor("d_out", [128, 8, 256], BF16).ap()
    d_wdo = nc.alloc_sbuf_tensor("d_wdo", [128, 4, 512], BF16).ap()
    ist_sb = nc.alloc_sbuf_tensor("ist", [128, 4, 512], BF16).ap()
    gel_sb = nc.alloc_sbuf_tensor("gel", [128, 4, 512], F32).ap()

    d_sc = carve(aren, 8192, 4096, F32, [4, 256])
    pb = [nc.alloc_psum_tensor(f"pb{i}", [128, 512], F32) for i in range(8)]

    em.op("vector", lambda v: v.memset(ones_sb, 1.0))
    ones_dv = em.op("vector", lambda v: v.memset(eps_sb, 1e-5), inc="SDV")
    em.dma("gpsimd", idb_sb, idb_i, "GCAST")
    wq_sb = carve(big0, 0, 65536, BF16, [HC, D])
    wk_sb = carve(big1, 0, 65536, BF16, [HC, D])

    def wrows(w_ap):
        return bass.AP(w_ap.tensor, 0, [[D, 128], [128 * D, HC], [1, D]])
    em.dma("gpsimd", wq_sb, wrows(wq_i), "GCAST")
    wqk_ld = em.dma("gpsimd", wk_sb, wrows(wk_i), "GCAST")

    class Decode:

        def __init__(self):
            self.jobs = []
            for q in range(32):
                for m in range(8):
                    self.jobs.append(("g", m, q))
            for q in range(32):
                for m in range(8):
                    self.jobs.append(("u", m, q))
            for q in range(4):
                for m in range(64):
                    self.jobs.append(("d", m, q))
            self.k = 0
            self.wr_tgt = [0, 0]
            self.dv_tgt = [0, 0]
            self.prev_dv = 0
            self.relu_war = 0

        def done(self):
            return self.k >= len(self.jobs)

        def emit(self, n):
            for _ in range(n):
                if self.done():
                    return
                self._one(self.k)
                self.k += 1

        def _one(self, k):
            kind, m, q = self.jobs[k]
            p = k % 2
            din = d_in[:, p, :]
            if self.dv_tgt[p]:
                em.wait("sync", "SDV", self.dv_tgt[p])
            dsem = f"GDEC{p}"
            if kind in ("g", "u"):
                src = pg16 if kind == "g" else pu16
                scs = sg_s if kind == "g" else su_s
                em.dma("sync", din, src[128 * m:128 * m + 128, 256 * q:256 * q + 256], dsem)
                ld2 = 0
                for pn in range(2):
                    sap = bass.AP(scs[pn].tensor, 8192 * 8 * m + 256 * q,
                                  [[8192, 8], [0, 16], [1, 256]])
                    ld2 = em.dma("sync", d_sc[:, 2 * p + pn, :], sap, dsem)
            else:
                em.dma("sync", din, pd16[128 * m:128 * m + 128, 256 * q:256 * q + 256], dsem)
                ld2 = 0
                for ph in range(2):
                    sap = bass.AP(sd_s[ph].tensor, 1024 * 8 * m + 256 * q,
                                  [[1024, 8], [0, 16], [1, 256]])
                    ld2 = em.dma("sync", d_sc[:, 2 * p + ph, :], sap, dsem)
            em.wait("vector", dsem, ld2)
            if self.wr_tgt[p]:
                em.wait("vector", f"GDWR{p}", self.wr_tgt[p])

            e1 = 0
            for nib in range(4):
                e1 = em.op("vector", lambda v, sh=4 * nib, nib=nib, din=din: v.tensor_scalar(
                    d_t[:, nib, :], din, sh, 7, ALU.logical_shift_right, ALU.bitwise_and),
                    inc="SDV")
            em.wait("vector", "SDV", e1)
            e2 = 0
            for nib in range(4):
                e2 = em.op("vector", lambda v, nib=nib: v.tensor_scalar(
                    d_mag[:, nib, :], d_t[:, nib, :], 64, 16128, ALU.mult, ALU.add), inc="SDV")
            e3 = 0
            for nib in range(4):
                msk = 8 << (4 * nib)
                if msk >= 32768:
                    msk -= 65536
                e3 = em.op("vector", lambda v, msk=msk, shl=12 - 4 * nib, nib=nib, din=din:
                           v.tensor_scalar(d_sb2[:, nib, :], din, msk, shl,
                                           ALU.bitwise_and, ALU.logical_shift_left), inc="SDV")
            em.wait("scalar", "SDV", e2)
            em.wait("scalar", "SDV", self.relu_war)
            a1 = 0
            for nib in range(4):
                a1 = em.op("scalar", lambda s2, nib=nib: s2.activation(
                    out=d_t3[:, nib, :], in_=d_mag[:, nib, :].bitcast(BF16),
                    func=AF.Relu, scale=-1.0, bias=1.0), inc="SACT")
            em.wait("vector", "SACT", a1)
            e4 = 0
            for nib in range(4):
                e4 = em.op("vector", lambda v, nib=nib: v.tensor_tensor(
                    d_t[:, nib, :].bitcast(BF16),
                    d_mag[:, nib, :].bitcast(BF16),
                    d_t3[:, nib, :], ALU.subtract), inc="SDV")
            self.relu_war = e4
            em.wait("vector", "SDV", e4)
            em.wait("vector", "SDV", e3)
            e5 = 0
            for nib in range(4):
                e5 = em.op("vector", lambda v, nib=nib: v.tensor_tensor(
                    d_mag[:, nib, :], d_t[:, nib, :], d_sb2[:, nib, :], ALU.bitwise_or), inc="SDV")
            em.wait("vector", "SDV", e5)
            wlast = [0, 0, 0, 0]
            for nib in range(4):
                if kind in ("g", "u"):
                    ph, pn = (nib % 2), (nib // 2)
                    scl = d_sc[:, 2 * p + pn, :]
                    oview = d_out[:, 4 * p + nib, :]
                else:
                    ph, pn = (nib // 2), (nib % 2)
                    scl = d_sc[:, 2 * p + ph, :]
                    oview = d_wdo[:, 2 * p + pn, :].rearrange("a (b c) -> a b c", c=2)[:, :, ph]
                wlast[nib] = em.op("vector", lambda v, nib=nib, oview=oview, scl=scl:
                                   v.tensor_tensor(
                    oview, d_mag[:, nib, :].bitcast(BF16),
                    scl, ALU.mult), inc="SDV")
            self.dv_tgt[p] = wlast[3]
            self.prev_dv = wlast[3]
            wr = 0
            if kind in ("g", "u"):
                dst = sc["wg_dec"] if kind == "g" else sc["wu_dec"]
                for nib in range(4):
                    ph, pn = (nib % 2), (nib // 2)
                    hcx = 2 * m + ph
                    em.wait("scalar", "SDV", wlast[nib])
                    dap = bass.AP(dst.tensor, hcx * 128 * NMLP + 512 * q + 128 * pn,
                                  [[NMLP, 128], [256, 2], [1, 128]])
                    wr = em.dma("scalar", dap, d_out[:, 4 * p + nib, :], f"GDWR{p}")
            else:
                em.wait("scalar", "SDV", wlast[3])
                for pn in range(2):
                    r0 = 256 * m + 128 * pn
                    wr = em.dma("scalar", sc["wd_dec"][r0:r0 + 128, 512 * q:512 * q + 512],
                                d_wdo[:, 2 * p + pn, :], f"GDWR{p}")
            self.wr_tgt[p] = wr

    dec = Decode()

    xn_tm = carve(aren, 0, 8192, BF16, [2, 2048])
    xn_sb = carve(aren, 16384, 8192, BF16, [2, HC, 128])
    qs_sb = carve(aren, 24576, 8192, BF16, [2, D])

    qk_evac_dv = [0, 0, 0, 0]
    xn_pe = [0, 0]
    xn_wr = [0, 0]
    xntm_pe = [0, 0]
    qs_wr = [0, 0]
    xrow_act = 0
    xrow_dv = 0
    var_par = [0, 0]
    tcopy_act = [0, 0, 0, 0]

    stats_par = [stat_sb, bst_sb]

    for st in range(NSTRIP):
        tok0 = st * 128
        pp = st % 2
        stp = stats_par[pp]
        em.wait("sync", "SACT", xrow_act)
        em.wait("sync", "SDV", xrow_dv)
        ld_r = em.dma("sync", xrow_sb, x_tm[tok0:tok0 + 128, :], "LXR")

        em.wait("vector", "LXR", ld_r)
        rs_dv = em.op("vector", lambda v, stp=stp: v.reduce_sum(
            stp[:, 0:1], xrow_sb, axis=AX.X), inc="SDV")
        em.wait("scalar", "LXR", ld_r)
        em.wait("scalar", "SDV", var_par[pp])
        if st == 0:
            em.wait("scalar", "SDV", ones_dv)
        sq1_act = em.op("scalar", lambda s2, stp=stp: s2.activation(
            out=sqs_sb, in_=xrow_sb, func=AF.Square, accum_out=stp[:, 1:2]), inc="SACT")
        xrow_act = sq1_act
        em.wait("vector", "SDV", rs_dv)
        mean_dv = em.op("vector", lambda v, stp=stp: v.tensor_scalar(
            stp[:, 2:3], stp[:, 0:1], 1.0 / D, None, ALU.mult), inc="SDV")
        em.wait("vector", "SDV", mean_dv)
        m2_dv = em.op("vector", lambda v, stp=stp: v.tensor_scalar(
            stp[:, 3:4], stp[:, 2:3], stp[:, 2:3], None, ALU.mult), inc="SDV")
        em.wait("vector", "SDV", m2_dv)
        em.wait("vector", "SACT", sq1_act)
        var_dv = em.op("vector", lambda v, stp=stp: v.scalar_tensor_tensor(
            stp[:, 4:5], stp[:, 1:2], 1.0 / D, stp[:, 3:4],
            ALU.mult, ALU.subtract), inc="SDV")
        var_par[pp] = var_dv
        em.wait("scalar", "SDV", var_dv)
        sd_act = em.op("scalar", lambda s2, stp=stp: s2.activation(
            out=stp[:, 5:6], in_=stp[:, 4:5], func=AF.Sqrt,
            bias=eps_sb[:, 0:1]), inc="SACT")
        em.wait("vector", "SACT", sd_act)
        rc_dv = em.op("vector", lambda v, stp=stp: v.reciprocal(
            stp[:, 3:4], stp[:, 5:6]), inc="SDV")

        em.wait("vector", "SDV", rc_dv)
        em.wait("vector", "SPE", xntm_pe[pp])
        in4 = xrow_sb.rearrange("p (b i q) -> p b q i", b=8, i=128, q=2)
        out4 = xn_tm[:, pp].rearrange("p (b q i) -> p b q i", b=8, q=2, i=128)
        xntm_dv = em.op("vector", lambda v, in4=in4, out4=out4, stp=stp: v.tensor_scalar(
            out4, in4, stp[:, 2:3], stp[:, 3:4], ALU.subtract, ALU.mult), inc="SDV")
        xrow_dv = xntm_dv

        dec.emit(8)
        em.wait("tensor", "SDV", xntm_dv)
        if st == 0:
            em.wait("tensor", "GCAST", 32)
        cp_act = 0
        for hc in range(HC):
            bk = hc % 4
            tp_b = pb[bk].ap().bitcast(BF16)[:, 0:128]
            em.wait("tensor", "SACT", tcopy_act[bk])
            tpe = em.op("tensor", lambda tn, hc=hc, pp=pp, tp_b=tp_b: tn.transpose(
                tp_b, xn_tm[:, pp, 128 * hc:128 * hc + 128], idb_sb), inc="SPE")
            if hc == HC - 1:
                xntm_pe[pp] = tpe
            em.wait("scalar", "SPE", tpe)
            if hc == 0:
                em.wait("scalar", "SPE", xn_pe[pp])
                em.wait("scalar", f"GXW{pp}", xn_wr[pp])
            cp_act = em.op("scalar", lambda s2, hc=hc, pp=pp, tp_b=tp_b: s2.copy(
                xn_sb[:, pp, hc, :], tp_b), inc="SACT")
            tcopy_act[bk] = cp_act
        xn_ready = cp_act

        if st < 32:
            em.wait("gpsimd", "SACT", xn_ready)
            dstap = bass.AP(sc["xn_sc"].tensor, tok0, [[TLOC, 128], [128 * TLOC, HC], [1, 128]])
            xn_wr[pp] = em.dma("gpsimd", dstap, xn_sb[:, pp], f"GXW{pp}")

        em.wait("tensor", "SACT", xn_ready)
        if st == 0:
            em.wait("tensor", "GCAST", wqk_ld)
        for kq in range(2):
            wsb = wq_sb if kq == 0 else wk_sb
            mm = None
            for hc in range(HC):
                for e5 in range(4):
                    if hc == 0:
                        em.wait("tensor", "SDV", qk_evac_dv[e5])
                    last = (hc == HC - 1 and e5 == 3)
                    mm = em.op("tensor", lambda tn, wsb=wsb, hc=hc, e5=e5, pp=pp: tn.matmul(
                        pb[4 + e5][:, :], xn_sb[:, pp, hc, :],
                        wsb[:, hc, 512 * e5:512 * e5 + 512],
                        start=(hc == 0), stop=(hc == HC - 1)),
                        inc="SPE" if last else None)
            pe_qk = mm
            if kq == 1:
                xn_pe[pp] = pe_qk
            em.wait("vector", "SPE", pe_qk)
            em.wait("vector", f"GQW{kq}", qs_wr[kq])
            for e5 in range(4):
                qk_evac_dv[e5] = em.op("vector", lambda v, e5=e5, kq=kq: v.tensor_copy(
                    qs_sb[:, kq, 512 * e5:512 * e5 + 512], pb[4 + e5][:, :]), inc="SDV")
            em.wait("gpsimd", "SDV", qk_evac_dv[3])
            dst = sc["q_sc"] if kq == 0 else sc["k_sc"]
            qs_wr[kq] = em.dma("gpsimd", dst[tok0:tok0 + 128, :], qs_sb[:, kq], f"GQW{kq}")

    a_pe = em.now("SPE")
    dec_wr_all = (em.now("GDWR0"), em.now("GDWR1"))
    if last_phase == 'A':
        return _finish(nc, em, dbg, dbg_outs, sc)

    q_sb = carve(big0, 0, 65536, BF16, [64, 512])
    k_sb = carve(big1, 0, 32768, BF16, [2, 8, 1024])
    sco_sb = carve(big1, 32768, 16384, F32, [2, 2048])
    exp_sb = carve(big1, 49152, 16384, F32, [2, 2048])
    asb = carve(aren, 0, 8192, BF16, [2, 2048])

    blk_ev = [0, 0]
    k_pe = [0, 0]
    a_wr = 0
    bi_g = 0
    es_prev_pe = a_pe

    for es in range(4):
        em.wait("sync", "SPE", es_prev_pe)
        if es == 0:
            em.wait("sync", "GQW0", qs_wr[0])
            em.wait("sync", "GQW1", qs_wr[1])
        ldq = 0
        for g8 in range(8):
            qap = bass.AP(sc["q_sc"].tensor, (8 * g8) * 128 * D + 512 * es,
                          [[D, 128], [D * 128, 8], [1, 512]])
            ldq = em.dma("sync", q_sb[:, 8 * g8:8 * g8 + 8, :], qap, "LQS")
        em.wait("tensor", "LQS", ldq)
        for ep in range(2):
            for fh in range(2):
                seti = bi_g % 2
                bks = [pb[4 * seti + i] for i in range(4)]
                mm = None
                for sc8 in range(8):
                    kp = sc8 % 2
                    em.wait("sync", "SPE", k_pe[kp])
                    kap = bass.AP(sc["k_sc"].tensor, (1024 * sc8) * D + 1024 * fh,
                                  [[D, 128], [D * 128, 8], [1, 1024]])
                    ldk = em.dma("sync", k_sb[:, kp], kap, f"LKC{kp}")
                    em.wait("tensor", f"LKC{kp}", ldk)
                    if sc8 == 0:
                        em.wait("tensor", "SDV", blk_ev[seti])
                    for stt in range(8):
                        for e1 in range(2):
                            for f5 in range(2):
                                last = (stt == 7 and e1 == 1 and f5 == 1)
                                mm = em.op("tensor", lambda tn, stt=stt, e1=e1, f5=f5,
                                           sc8=sc8, ep=ep, kp=kp, bks=bks: tn.matmul(
                                    bks[2 * e1 + f5][:, :],
                                    q_sb[:, 8 * sc8 + stt,
                                         128 * (2 * ep + e1):128 * (2 * ep + e1) + 128],
                                    k_sb[:, kp, stt, 512 * f5:512 * f5 + 512],
                                    start=(sc8 == 0 and stt == 0),
                                    stop=(sc8 == 7 and stt == 7)),
                                    inc="SPE" if last else None)
                    k_pe[kp] = mm
                pe_blk = k_pe[1]
                em.wait("vector", "SPE", pe_blk)
                for e1 in range(2):
                    for f5 in range(2):
                        blk_ev[seti] = em.op("vector", lambda v, e1=e1, f5=f5, fh=fh,
                                             bks=bks: v.tensor_copy(
                            sco_sb[:, e1, 1024 * fh + 512 * f5:1024 * fh + 512 * f5 + 512],
                            bks[2 * e1 + f5][:, :]), inc="SDV")
                bi_g += 1
            em.wait("vector", "SDV", blk_ev[0])
            em.wait("vector", "SDV", blk_ev[1])
            em.wait("vector", "GAW", a_wr)
            for e1 in range(2):
                mx_dv = em.op("vector", lambda v, e1=e1: v.reduce_max(
                    bst_sb[:, 0:1], sco_sb[:, e1, :], axis=AX.X), inc="SDV")
                em.wait("vector", "SDV", mx_dv)
                sm1_dv = em.op("vector", lambda v: v.tensor_scalar(
                    bst_sb[:, 1:2], bst_sb[:, 0:1], -0.125, None, ALU.mult), inc="SDV")
                em.wait("scalar", "SDV", sm1_dv)
                ex_act = em.op("scalar", lambda s2, e1=e1: s2.activation(
                    out=exp_sb[:, e1, :], in_=sco_sb[:, e1, :], func=AF.Exp,
                    bias=bst_sb[:, 1:2], scale=0.125, accum_out=bst_sb[:, 2:3]), inc="SACT")
                em.wait("vector", "SACT", ex_act)
                rc2_dv = em.op("vector", lambda v: v.reciprocal(
                    bst_sb[:, 3:4], bst_sb[:, 2:3]), inc="SDV")
                em.wait("vector", "SDV", rc2_dv)
                a_dv = em.op("vector", lambda v, e1=e1: v.tensor_scalar(
                    asb[:, e1, :], exp_sb[:, e1, :], bst_sb[:, 3:4], None, ALU.mult), inc="SDV")
                em.wait("gpsimd", "SDV", a_dv)
                erow = 512 * es + 256 * ep + 128 * e1
                a_wr = em.dma("gpsimd", sc["a_sc"][erow:erow + 128, :], asb[:, e1, :], "GAW")
            dec.emit(32)
        es_prev_pe = em.now("SPE")

    b_pe = em.now("SPE")
    if last_phase == 'B':
        dec.emit(999)
        return _finish(nc, em, dbg, dbg_outs, sc)

    wv_sb = carve(big0, 0, 65536, BF16, [HC, D])
    em.wait("gpsimd", "SPE", b_pe)
    wv_ld = em.dma("gpsimd", wv_sb, wrows(wv_i), "GCAST")
    at_sb = carve(big1, 0, 65536, BF16, [HC, 2048])
    vtb_sb = carve(aren, 0, 16384, BF16, [HC, 512])
    xnc = carve(aren, 16384, 16384, BF16, [HC, 512])
    arow = ist_sb.rearrange("p a b -> p (a b)")
    c2st = ist_sb

    tb_dv = [0, 0]
    at_ev = 0
    for ec in range(HC):
        if ec == 0:
            em.wait("sync", "GAW", a_wr)
            em.wait("sync", "SPE", b_pe)
        else:
            em.wait("sync", "SDV", at_ev)
        lda = em.dma("sync", arow, sc["a_sc"][128 * ec:128 * ec + 128, :], "LAR")
        em.wait("tensor", "LAR", lda)
        for fc in range(HC):
            tp_b = pb[4 + fc % 2].ap().bitcast(BF16)[:, 0:128]
            em.wait("tensor", "SDV", tb_dv[fc % 2])
            tpe = em.op("tensor", lambda tn, fc=fc, tp_b=tp_b: tn.transpose(
                tp_b, arow[:, 128 * fc:128 * fc + 128], idb_sb), inc="SPE")
            em.wait("vector", "SPE", tpe)
            tb_dv[fc % 2] = em.op("vector", lambda v, fc=fc, ec=ec, tp_b=tp_b: v.tensor_copy(
                at_sb[:, fc, 128 * ec:128 * ec + 128], tp_b), inc="SDV")
        at_ev = tb_dv[(HC - 1) % 2]
    at_done_dv = em.now("SDV")

    vban_dv = [0, 0, 0, 0]
    c2_wr = [0, 0]
    ob_dv = [0, 0, 0, 0]
    xnc_pe = 0
    for sb8 in range(8):
        em.wait("sync", "SPE", xnc_pe if sb8 > 0 else b_pe)
        if sb8 == 0:
            em.wait("sync", "SDV", at_done_dv)
        xap = bass.AP(sc["xn_sc"].tensor, 512 * sb8, [[TLOC, 128], [128 * TLOC, HC], [1, 512]])
        ldxn = em.dma("sync", xnc, xap, "LXC")
        em.wait("tensor", "LXC", ldxn)
        if sb8 == 0:
            em.wait("tensor", "GCAST", wv_ld)
            em.wait("tensor", "SDV", at_done_dv)
        for fc in range(HC):
            bk = fc % 4
            mm = None
            for hc in range(HC):
                if hc == 0:
                    em.wait("tensor", "SDV", vban_dv[bk])
                mm = em.op("tensor", lambda tn, fc=fc, hc=hc, bk=bk: tn.matmul(
                    pb[bk][:, :], wv_sb[:, hc, 128 * fc:128 * fc + 128],
                    xnc[:, hc, :], start=(hc == 0), stop=(hc == HC - 1)),
                    inc="SPE" if hc == HC - 1 else None)
            vpe = mm
            if fc == HC - 1:
                xnc_pe = vpe
            em.wait("vector", "SPE", vpe)
            vban_dv[bk] = em.op("vector", lambda v, fc=fc, bk=bk: v.tensor_copy(
                vtb_sb[:, fc, :], pb[bk][:, :]), inc="SDV")
        vt_dv = em.now("SDV")
        em.wait("tensor", "SDV", vt_dv)
        for ec in range(HC):
            bk = 4 + ec % 4
            mm = None
            for fc in range(HC):
                if fc == 0:
                    em.wait("tensor", "SDV", ob_dv[ec % 4])
                mm = em.op("tensor", lambda tn, ec=ec, fc=fc, bk=bk: tn.matmul(
                    pb[bk][:, :], at_sb[:, fc, 128 * ec:128 * ec + 128],
                    vtb_sb[:, fc, :], start=(fc == 0), stop=(fc == HC - 1)),
                    inc="SPE" if fc == HC - 1 else None)
            mpe = mm
            em.wait("vector", "SPE", mpe)
            em.wait("vector", f"GC2W{ec % 2}", c2_wr[ec % 2])
            ob_dv[ec % 4] = em.op("vector", lambda v, ec=ec, bk=bk: v.tensor_copy(
                c2st[:, ec % 2, :], pb[bk][:, :]), inc="SDV")
            em.wait("gpsimd", "SDV", ob_dv[ec % 4])
            off = 512 * sb8
            c2_wr[ec % 2] = em.dma("gpsimd", sc["attnT_sc"][ec, :, off:off + 512],
                                   c2st[:, ec % 2, :], f"GC2W{ec % 2}")
        dec.emit(26)

    dec.emit(999)
    c2_pe = em.now("SPE")
    c2_wr_all = (em.now("GC2W0"), em.now("GC2W1"))
    if last_phase == 'C':
        return _finish(nc, em, dbg, dbg_outs, sc)

    wo_sb = carve(big0, 0, 65536, BF16, [HC, D])
    em.wait("gpsimd", "SPE", c2_pe)
    wo_ld = em.dma("gpsimd", wo_sb, wrows(wo_i), "GCAST")
    atn_sb = carve(big1, 0, 32768, BF16, [2, HC, 512])
    af_st = carve(aren, 0, 8192, F32, [4, 512])

    atn_pe = [0, 0]
    c3ob_dv = [0, 0, 0, 0, 0, 0, 0, 0]
    af_wr = [0, 0]
    for s5 in range(8):
        ap_ = s5 % 2
        em.wait("sync", "SPE", atn_pe[ap_])
        if s5 == 0:
            em.wait("sync", "GC2W0", c2_wr_all[0])
            em.wait("sync", "GC2W1", c2_wr_all[1])
            em.wait("sync", "SPE", c2_pe)
        aap = bass.AP(sc["attnT_sc"].tensor, 512 * s5,
                      [[TLOC, 128], [128 * TLOC, HC], [1, 512]])
        lda = em.dma("sync", atn_sb[:, ap_], aap, f"LAT{ap_}")
        em.wait("tensor", f"LAT{ap_}", lda)
        if s5 == 0:
            em.wait("tensor", "GCAST", wo_ld)
        bset = 4 * (s5 % 2)
        for st1 in range(4):
            mm = None
            for ec in range(HC):
                for g5 in range(4):
                    if ec == 0:
                        em.wait("tensor", "SDV", c3ob_dv[bset + g5])
                    last = (ec == HC - 1 and g5 == 3)
                    mm = em.op("tensor", lambda tn, ap_=ap_, st1=st1, ec=ec, g5=g5,
                               bset=bset: tn.matmul(
                        pb[bset + g5][:, :],
                        atn_sb[:, ap_, ec, 128 * st1:128 * st1 + 128],
                        wo_sb[:, ec, 512 * g5:512 * g5 + 512],
                        start=(ec == 0), stop=(ec == HC - 1)),
                        inc="SPE" if last else None)
            mpe = mm
            atn_pe[ap_] = mpe
            em.wait("vector", "SPE", mpe)
            em.wait("vector", f"GAFW{s5 % 2}", af_wr[s5 % 2])
            for g5 in range(4):
                c3ob_dv[bset + g5] = em.op("vector", lambda v, g5=g5, bset=bset: v.tensor_copy(
                    af_st[:, g5, :], pb[bset + g5][:, :]), inc="SDV")
            em.wait("gpsimd", "SDV", c3ob_dv[bset + 3])
            tok0 = 512 * s5 + 128 * st1
            af_wr[s5 % 2] = em.dma("gpsimd", sc["afin_sc"][tok0:tok0 + 128, :],
                                   af_st.rearrange("p a b -> p (a b)"), f"GAFW{s5 % 2}")

    c3_pe = em.now("SPE")
    c3_wr_all = (em.now("GAFW0"), em.now("GAFW1"))
    if last_phase == 'C3':
        return _finish(nc, em, dbg, dbg_outs, sc)

    xnh_sb = carve(big0, 0, 65536, BF16, [HC, 2048])
    wgu_sb = carve(big1, 0, 32768, BF16, [2, 2, HC, 256])

    gel_dv = [0, 0, 0, 0]
    sig_war = [0, 0, 0, 0]
    ist_wr = [0, 0, 0, 0]
    wgu_pe = [0, 0]
    d_first = True
    for dh in range(2):
        em.wait("sync", "SPE", c3_pe if dh == 0 else em.now("SPE"))
        ldh = 0
        for hc in range(HC):
            xap = bass.AP(sc["xn_sc"].tensor, hc * 128 * TLOC + 2048 * dh,
                          [[TLOC, 128], [1, 2048]])
            ldh = em.dma("sync", xnh_sb[:, hc, :], xap, "LXH")
        for ntp in range(64):
            wp = ntp % 2
            em.wait("sync", "SPE", wgu_pe[wp])
            if d_first:
                em.wait("sync", "GDWR0", dec_wr_all[0])
                em.wait("sync", "GDWR1", dec_wr_all[1])
                d_first = False
            gap = bass.AP(sc["wg_dec"].tensor, 256 * ntp, [[NMLP, 128], [128 * NMLP, HC], [1, 256]])
            uap = bass.AP(sc["wu_dec"].tensor, 256 * ntp, [[NMLP, 128], [128 * NMLP, HC], [1, 256]])
            em.dma("sync", wgu_sb[:, 0, wp], gap, f"LWG{wp}")
            ldw = em.dma("sync", wgu_sb[:, 1, wp], uap, f"LWG{wp}")
            for nt1 in range(2):
                nt = 2 * ntp + nt1
                em.wait("tensor", f"LWG{wp}", ldw)
                if ntp == 0 and nt1 == 0:
                    em.wait("tensor", "LXH", ldh)
                gpe = upe = None
                for gu in range(2):
                    mm = None
                    for hc in range(HC):
                        for t5 in range(4):
                            if hc == 0:
                                em.wait("tensor", "SDV", gel_dv[t5])
                            last = (hc == HC - 1 and t5 == 3)
                            mm = em.op("tensor", lambda tn, gu=gu, hc=hc, t5=t5, wp=wp,
                                       nt1=nt1: tn.matmul(
                                pb[4 * gu + t5][:, :],
                                wgu_sb[:, gu, wp, hc, 128 * nt1:128 * nt1 + 128],
                                xnh_sb[:, hc, 512 * t5:512 * t5 + 512],
                                start=(hc == 0), stop=(hc == HC - 1)),
                                inc="SPE" if last else None)
                    if gu == 0:
                        gpe = mm
                    else:
                        upe = mm
                wgu_pe[wp] = upe
                for t5 in range(4):
                    gb, ub = pb[t5], pb[4 + t5]
                    gslot = gel_sb[:, t5, :]
                    qslot = sqs_sb[:, 512 * t5:512 * t5 + 512]
                    em.wait("scalar", "SPE", gpe)
                    em.wait("scalar", "SDV", gel_dv[t5])
                    s2m = em.op("scalar", lambda s2_, gb=gb, gslot=gslot: s2_.activation(
                        out=gslot, in_=gb[:, :], func=AF.Square), inc="SACT")
                    em.wait("vector", "SACT", s2m)
                    em.wait("vector", f"GIW{t5}", ist_wr[t5])
                    uum = em.op("vector", lambda v, gslot=gslot: v.tensor_scalar(
                        gslot, gslot, GELU_B, 1.0, ALU.mult, ALU.add), inc="SDV")
                    em.wait("vector", "SDV", uum)
                    w1m = em.op("vector", lambda v, gb=gb, gslot=gslot: v.tensor_tensor(
                        gslot, gslot, gb[:, :], ALU.mult), inc="SDV")
                    em.wait("scalar", "SDV", w1m)
                    em.wait("scalar", "SDV", sig_war[t5])
                    sgm = em.op("scalar", lambda s2, gslot=gslot, qslot=qslot: s2.activation(
                        out=qslot, in_=gslot, func=AF.Sigmoid, scale=GELU_A), inc="SACT")
                    em.wait("vector", "SACT", sgm)
                    w2m = em.op("vector", lambda v, gb=gb, gslot=gslot, qslot=qslot:
                                v.tensor_tensor(gslot, qslot, gb[:, :], ALU.mult), inc="SDV")
                    sig_war[t5] = w2m
                    em.wait("vector", "SDV", w2m)
                    em.wait("vector", "SPE", upe)
                    gel_dv[t5] = em.op("vector", lambda v, ub=ub, t5=t5, gslot=gslot:
                                       v.tensor_tensor(
                        ist_sb[:, t5, :], gslot, ub[:, :], ALU.mult), inc="SDV")
                    em.wait("gpsimd", "SDV", gel_dv[t5])
                    off = 2048 * dh + 512 * t5
                    ist_wr[t5] = em.dma("gpsimd", sc["inter_sc"][nt, :, off:off + 512],
                                        ist_sb[:, t5, :], f"GIW{t5}")

    d_pe = em.now("SPE")
    d_wr_all = tuple(em.now(f"GIW{i}") for i in range(4))
    if last_phase == 'D':
        return _finish(nc, em, dbg, dbg_outs, sc)

    wdt_lo = carve(big0, 0, 65536, BF16, [64, 512])
    wdt_hi = carve(big1, 0, 65536, BF16, [64, 512])

    def wdt(nt):
        return wdt_lo[:, nt, :] if nt < 64 else wdt_hi[:, nt - 64, :]

    ich_sb = carve(aren, 0, 32768, BF16, [2, HC, 512])
    xs_sb = xrow_sb.rearrange("p (a b) -> p a b", a=4)
    as_sb = sqs_sb.rearrange("p (a b) -> p a b", a=4)

    e_first = True
    ich_pe = [0, 0]
    eob_dv = [0, 0, 0, 0, 0, 0, 0, 0]
    o_wr = [0, 0]
    xa_dv = 0
    wdt_pe = 0
    for hp in range(4):
        em.wait("sync", "SPE", wdt_pe)
        if e_first:
            em.wait("sync", "SPE", d_pe)
            for i4 in range(4):
                em.wait("sync", f"GIW{i4}", d_wr_all[i4])
            e_first = False
        ldwd = 0
        for i in range(16):
            wap = bass.AP(sc["wd_dec"].tensor, (1024 * i) * D + 512 * hp,
                          [[D, 128], [D * 128, 8], [1, 512]])
            ldwd = em.dma("sync",
                          (wdt_lo[:, 8 * i:8 * i + 8, :] if i < 8
                           else wdt_hi[:, 8 * (i - 8):8 * (i - 8) + 8, :]),
                          wap, "LWD")
        for t5 in range(8):
            bset = 4 * (t5 % 2)
            em.wait("sync", "SDV", xa_dv)
            if hp == 0 and t5 == 0:
                em.wait("sync", "GAFW0", c3_wr_all[0])
                em.wait("sync", "GAFW1", c3_wr_all[1])
            xap = bass.AP(x_tm.tensor, (512 * t5) * D + 512 * hp,
                          [[D, 128], [D * 128, 4], [1, 512]])
            aap = bass.AP(sc["afin_sc"].tensor, (512 * t5) * D + 512 * hp,
                          [[D, 128], [D * 128, 4], [1, 512]])
            em.dma("sync", xs_sb, xap, "LXA")
            ldxa = em.dma("sync", as_sb, aap, "LXA")
            mm = None
            for e8 in range(8):
                ip = e8 % 2
                em.wait("sync", "SPE", ich_pe[ip])
                iap = bass.AP(sc["inter_sc"].tensor, (16 * e8) * 128 * TLOC + 512 * t5,
                              [[TLOC, 128], [128 * TLOC, HC], [1, 512]])
                ldi = em.dma("sync", ich_sb[:, ip], iap, f"LIC{ip}")
                em.wait("tensor", f"LIC{ip}", ldi)
                if e8 == 0:
                    em.wait("tensor", "LWD", ldwd)
                    for t1 in range(4):
                        em.wait("tensor", "SDV", eob_dv[bset + t1])
                for ntl in range(HC):
                    nt = 16 * e8 + ntl
                    for t1 in range(4):
                        last = (ntl == HC - 1 and t1 == 3)
                        mm = em.op("tensor", lambda tn, ip=ip, ntl=ntl, nt=nt, t1=t1,
                                   bset=bset: tn.matmul(
                            pb[bset + t1][:, :],
                            ich_sb[:, ip, ntl, 128 * t1:128 * t1 + 128],
                            wdt(nt), start=(nt == 0), stop=(nt == 127)),
                            inc="SPE" if last else None)
                ich_pe[ip] = mm
            mpe = ich_pe[1]
            if t5 == 7:
                wdt_pe = mpe
            em.wait("vector", "SPE", mpe)
            em.wait("vector", "LXA", ldxa)
            for t1 in range(4):
                osl = 2 + t1 % 2
                em.wait("vector", f"GOW{t1 % 2}", o_wr[t1 % 2])
                r1 = em.op("vector", lambda v, t1=t1, bset=bset: v.tensor_tensor(
                    gel_sb[:, 0, :], as_sb[:, t1, :], pb[bset + t1][:, :], ALU.add), inc="SDV")
                em.wait("vector", "SDV", r1)
                eob_dv[bset + t1] = em.op("vector", lambda v, t1=t1, osl=osl: v.tensor_tensor(
                    gel_sb[:, osl, :], gel_sb[:, 0, :], xs_sb[:, t1, :], ALU.add), inc="SDV")
                em.wait("gpsimd", "SDV", eob_dv[bset + t1])
                tok0 = 512 * t5 + 128 * t1
                o_wr[t1 % 2] = em.dma("gpsimd",
                                      out_loc[tok0:tok0 + 128, 512 * hp:512 * hp + 512],
                                      gel_sb[:, osl, :], f"GOW{t1 % 2}")
            xa_dv = eob_dv[bset + 3]

    return _finish(nc, em, dbg, dbg_outs, sc)


def _finish(nc, em, dbg, dbg_outs, sc):
    for name in dbg:
        em.dma("gpsimd", dbg_outs[name], sc[name], "GDBG")
    for s_ in em.dma_sems:
        eng = "gpsimd" if s_.startswith("G") else "sync"
        em.wait(eng, s_, em.now(s_))
    with nc.Block() as block:
        @block.sync
        def _(e):
            for f in em.streams["sync"]:
                f(e)

        @block.gpsimd
        def _(e):
            for f in em.streams["gpsimd"]:
                f(e)

        @block.scalar
        def _(e):
            for f in em.streams["scalar"]:
                f(e)

        @block.vector
        def _(e):
            for f in em.streams["vector"]:
                f(e)

        @block.tensor
        def _(e):
            for f in em.streams["tensor"]:
                f(e)
    counts = {k: len(v) for k, v in em.streams.items()}
    return nc, counts


def host_inputs(inp):
    import ml_dtypes
    perm = _perm_h()
    x = np.asarray(inp["L0_attn_x"], dtype=np.float32)
    wq = np.asarray(inp["L0_attn_Wq"], dtype=np.float32)
    wk = np.asarray(inp["L0_attn_Wk"], dtype=np.float32)
    wv = np.asarray(inp["L0_attn_Wv"], dtype=np.float32)
    wo = np.asarray(inp["L0_attn_Wo"], dtype=np.float32)

    def pack_t(p_int32):
        b = np.ascontiguousarray(p_int32).view(np.uint8).reshape(
            p_int32.shape[0], p_int32.shape[1], 4)[:, :, 0]
        bt = np.ascontiguousarray(b.T)
        return bt.view("<i2")

    pg = pack_t(np.asarray(inp["L0_ffn_gate_W_packed"]))
    pu = pack_t(np.asarray(inp["L0_ffn_up_W_packed"]))
    pd = pack_t(np.asarray(inp["L0_ffn_down_W_packed"]))
    sgt = np.ascontiguousarray(np.asarray(inp["L0_ffn_gate_scales"], dtype=np.float32).T)
    sut = np.ascontiguousarray(np.asarray(inp["L0_ffn_up_scales"], dtype=np.float32).T)
    sdt = np.ascontiguousarray(np.asarray(inp["L0_ffn_down_scales"], dtype=np.float32).T)
    sg0 = np.ascontiguousarray(sgt[:, 0::2])
    sg1 = np.ascontiguousarray(sgt[:, 1::2])
    su0 = np.ascontiguousarray(sut[:, 0::2])
    su1 = np.ascontiguousarray(sut[:, 1::2])
    sd0 = np.ascontiguousarray(sdt[:, 0::2])
    sd1 = np.ascontiguousarray(sdt[:, 1::2])
    shared = {
        "wq": np.ascontiguousarray(wq[perm]),
        "wk": np.ascontiguousarray(wk[perm]),
        "wv": np.ascontiguousarray(wv[perm]),
        "wo": np.ascontiguousarray(wo),
        "idb": np.eye(128, dtype=np.float32).astype(ml_dtypes.bfloat16),
    }
    maps = []
    for c in range(8):
        b, h = c // 2, c % 2
        lf = np.concatenate([np.arange(h * 4096, (h + 1) * 4096),
                             np.arange((1 - h) * 4096, (2 - h) * 4096)])
        xb = x[b][lf]
        m = dict(shared)
        m["x_tm"] = np.ascontiguousarray(xb)
        m["pg16"] = pg
        m["pu16"] = pu
        m["pd16"] = pd
        m["sg0"] = sg0
        m["sg1"] = sg1
        m["su0"] = su0
        m["su1"] = su1
        m["sd0"] = sd0
        m["sd1"] = sd1
        maps.append(m)
    return maps


def kernel(**inputs):
    nc, _ = build_program(dbg=DBG)
    maps = host_inputs(inputs)
    res = run_bass_kernel_spmd(nc, maps, core_ids=list(range(8)))
    out = np.empty((B_, S, D), dtype=np.float32)
    for c in range(8):
        b, h = c // 2, c % 2
        out[b, h * 4096:(h + 1) * 4096, :] = res.results[c]["out_loc"]
    return out


if __name__ == "__main__":
    nc, counts = build_program()
    print("instruction closures per engine:", counts)


# version 12
# speedup vs baseline: 1.0144x; 1.0144x over previous
import numpy as np
import concourse.bass as bass
import concourse.mybir as mybir
from concourse.bass_utils import run_bass_kernel_spmd

F32 = mybir.dt.float32
BF16 = mybir.dt.bfloat16
I16 = mybir.dt.int16
U8 = mybir.dt.uint8
AF = mybir.ActivationFunctionType
ALU = mybir.AluOpType
AX = mybir.AxisListType

B_, S, D = 4, 8192, 2048
NMLP = 16384
TLOC = 4096
HC = 16
NSTRIP = 64
GELU_A = 1.5957691216056908
GELU_B = 0.044715

DBG = []


def _perm_h():
    p = []
    for m in range(8):
        p += [256 * m + 2 * i for i in range(128)]
        p += [256 * m + 2 * i + 1 for i in range(128)]
    return np.array(p, dtype=np.int64)


class Emit:
    def __init__(self, nc):
        self.nc = nc
        self.streams = {k: [] for k in ("sync", "gpsimd", "scalar", "vector", "tensor")}
        self.sems = {}
        self.cnt = {}

    def sem(self, name):
        h = self.nc.semaphore(name).__enter__()
        self.sems[name] = h
        self.cnt[name] = 0

    def add(self, eng, fn):
        self.streams[eng].append(fn)

    def dma(self, eng, out_ap, in_ap, sem_name):
        h = self.sems[sem_name]
        self.cnt[sem_name] += 16
        tgt = self.cnt[sem_name]

        def _f(e, out_ap=out_ap, in_ap=in_ap, h=h):
            e.dma_start(out=out_ap, in_=in_ap).then_inc(h, 16)
        self.add(eng, _f)
        return tgt

    def op(self, eng, builder, inc=None):
        tgt, h = None, None
        if inc is not None:
            self.cnt[inc] += 1
            tgt = self.cnt[inc]
            h = self.sems[inc]

        def _f(e, builder=builder, h=h):
            ins = builder(e)
            if h is not None:
                ins.then_inc(h, 1)
        self.add(eng, _f)
        return tgt

    def wait(self, eng, sem_name, val):
        if not val:
            return
        h = self.sems[sem_name]

        def _f(e, h=h, val=val):
            e.wait_ge(h, val)
        self.add(eng, _f)

    def now(self, sem_name):
        return self.cnt[sem_name]


def build_program(dbg=(), last_phase='E'):
    nc = bass.Bass("TRN2", target_bir_lowering=False)
    em = Emit(nc)
    DMA_SEMS = ["LXT0", "LXT1", "LXR", "LQS", "LKC0", "LKC1", "LAR", "LXC",
                "LAT0", "LAT1", "LXH", "LWG0", "LWG1", "LWD", "LIC0", "LIC1", "LXA",
                "GCAST", "GDEC0", "GDEC1", "GDWR0", "GDWR1", "GXW0", "GXW1",
                "GQW0", "GQW1", "GAW", "GC2W0", "GC2W1", "GAFW0", "GAFW1",
                "GIW0", "GIW1", "GIW2", "GIW3", "GOW0", "GOW1", "GDBG", "GAGR"]
    for s_ in ["SPE", "SDV", "SACT"] + DMA_SEMS:
        em.sem(s_)
    em.dma_sems = DMA_SEMS

    x_tm = nc.dram_tensor("x_tm", [S, D], F32, kind="ExternalInput").ap()
    wq_i = nc.dram_tensor("wq", [D, D], F32, kind="ExternalInput").ap()
    wk_i = nc.dram_tensor("wk", [D, D], F32, kind="ExternalInput").ap()
    wv_i = nc.dram_tensor("wv", [D, D], F32, kind="ExternalInput").ap()
    wo_i = nc.dram_tensor("wo", [D, D], F32, kind="ExternalInput").ap()
    pg16 = nc.dram_tensor("pg16", [1024, 8192], I16, kind="ExternalInput").ap()
    pu16 = nc.dram_tensor("pu16", [1024, 8192], I16, kind="ExternalInput").ap()
    pd16 = nc.dram_tensor("pd16", [8192, 1024], I16, kind="ExternalInput").ap()
    sg_s = [nc.dram_tensor(f"sg{p}", [64, 8192], F32, kind="ExternalInput").ap() for p in range(2)]
    su_s = [nc.dram_tensor(f"su{p}", [64, 8192], F32, kind="ExternalInput").ap() for p in range(2)]
    sd_s = [nc.dram_tensor(f"sd{p}", [512, 1024], F32, kind="ExternalInput").ap() for p in range(2)]
    idb_i = nc.dram_tensor("idb", [128, 128], BF16, kind="ExternalInput").ap()
    out_loc = nc.dram_tensor("out_loc", [TLOC, D], F32, kind="ExternalOutput").ap()

    sc = {
        "xn_sc": nc.dram_tensor("xn_sc", [HC, 128, TLOC], BF16).ap(),
        "q_sc": nc.dram_tensor("q_sc", [S, D], BF16).ap(),
        "k_sc": nc.dram_tensor("k_sc", [S, D], BF16).ap(),
        "a_sc": nc.dram_tensor("a_sc", [D, D], BF16).ap(),
        "attnT_sc": nc.dram_tensor("attnT_sc", [HC, 128, TLOC], BF16).ap(),
        "afin_sc": nc.dram_tensor("afin_sc", [TLOC, D], F32).ap(),
        "wg_dec": nc.dram_tensor("wg_dec", [HC, 128, NMLP], BF16).ap(),
        "wu_dec": nc.dram_tensor("wu_dec", [HC, 128, NMLP], BF16).ap(),
        "wd_dec": nc.dram_tensor("wd_dec", [NMLP, D], BF16).ap(),
        "inter_sc": nc.dram_tensor("inter_sc", [128, 128, TLOC], BF16).ap(),
    }
    dbg_outs = {}
    for name in dbg:
        t = sc[name].tensor
        dbg_outs[name] = nc.dram_tensor("dbg_" + name, list(t.shape), t.dtype,
                                        kind="ExternalOutput").ap()

    big0 = nc.alloc_sbuf_tensor("big0", [128, 65536], U8).ap()
    big1 = nc.alloc_sbuf_tensor("big1", [128, 65536], U8).ap()
    aren = nc.alloc_sbuf_tensor("aren", [128, 32768], U8).ap()

    def carve(base, off, nbytes, dtype, shape):
        ap = base[:, off:off + nbytes].bitcast(dtype)
        if len(shape) == 1:
            return ap
        spec = "p (" + " ".join(f"d{i}" for i in range(len(shape))) + ") -> p " + \
               " ".join(f"d{i}" for i in range(len(shape)))
        return ap.rearrange(spec, **{f"d{i}": shape[i] for i in range(len(shape))})

    xrow_sb = nc.alloc_sbuf_tensor("xrow", [128, D], F32).ap()
    sqs_sb = nc.alloc_sbuf_tensor("sqs", [128, D], F32).ap()
    stat_sb = nc.alloc_sbuf_tensor("stat", [128, 8], F32).ap()
    bst_sb = nc.alloc_sbuf_tensor("bst", [128, 8], F32).ap()
    ones_sb = nc.alloc_sbuf_tensor("ones1", [1, 128], F32).ap()
    eps_sb = nc.alloc_sbuf_tensor("eps1", [128, 1], F32).ap()
    idb_sb = nc.alloc_sbuf_tensor("idb_sb", [128, 128], BF16).ap()
    d_in = nc.alloc_sbuf_tensor("d_in", [128, 2, 256], I16).ap()
    d_t = nc.alloc_sbuf_tensor("d_t", [128, 4, 256], I16).ap()
    d_mag = nc.alloc_sbuf_tensor("d_mag", [128, 4, 256], I16).ap()
    d_sb2 = nc.alloc_sbuf_tensor("d_sb2", [128, 4, 256], I16).ap()
    d_t3 = nc.alloc_sbuf_tensor("d_t3", [128, 4, 256], BF16).ap()
    d_out = nc.alloc_sbuf_tensor("d_out", [128, 8, 256], BF16).ap()
    d_wdo = nc.alloc_sbuf_tensor("d_wdo", [128, 4, 512], BF16).ap()
    ist_sb = nc.alloc_sbuf_tensor("ist", [128, 4, 512], BF16).ap()
    gel_sb = nc.alloc_sbuf_tensor("gel", [128, 4, 512], F32).ap()

    d_sc = carve(aren, 8192, 4096, F32, [4, 256])
    pb = [nc.alloc_psum_tensor(f"pb{i}", [128, 512], F32) for i in range(8)]

    em.op("vector", lambda v: v.memset(ones_sb, 1.0))
    ones_dv = em.op("vector", lambda v: v.memset(eps_sb, 1e-5), inc="SDV")
    em.dma("gpsimd", idb_sb, idb_i, "GCAST")
    wq_sb = carve(big0, 0, 65536, BF16, [HC, D])
    wk_sb = carve(big1, 0, 65536, BF16, [HC, D])

    def wrows(w_ap):
        return bass.AP(w_ap.tensor, 0, [[D, 128], [128 * D, HC], [1, D]])
    em.dma("gpsimd", wq_sb, wrows(wq_i), "GCAST")
    wqk_ld = em.dma("gpsimd", wk_sb, wrows(wk_i), "GCAST")

    class Decode:

        def __init__(self):
            self.jobs = []
            for q in range(32):
                for m in range(8):
                    self.jobs.append(("g", m, q))
            for q in range(32):
                for m in range(8):
                    self.jobs.append(("u", m, q))
            for q in range(4):
                for m in range(64):
                    self.jobs.append(("d", m, q))
            self.k = 0
            self.wr_tgt = [0, 0]
            self.dv_tgt = [0, 0]
            self.prev_dv = 0
            self.relu_war = 0

        def done(self):
            return self.k >= len(self.jobs)

        def emit(self, n):
            for _ in range(n):
                if self.done():
                    return
                self._one(self.k)
                self.k += 1

        def _one(self, k):
            kind, m, q = self.jobs[k]
            p = k % 2
            din = d_in[:, p, :]
            if self.dv_tgt[p]:
                em.wait("gpsimd", "SDV", self.dv_tgt[p])
            dsem = f"GDEC{p}"
            if kind in ("g", "u"):
                src = pg16 if kind == "g" else pu16
                scs = sg_s if kind == "g" else su_s
                em.dma("gpsimd", din, src[128 * m:128 * m + 128, 256 * q:256 * q + 256], dsem)
                ld2 = 0
                for pn in range(2):
                    sap = bass.AP(scs[pn].tensor, 8192 * 8 * m + 256 * q,
                                  [[8192, 8], [0, 16], [1, 256]])
                    ld2 = em.dma("gpsimd", d_sc[:, 2 * p + pn, :], sap, dsem)
            else:
                em.dma("gpsimd", din, pd16[128 * m:128 * m + 128, 256 * q:256 * q + 256], dsem)
                ld2 = 0
                for ph in range(2):
                    sap = bass.AP(sd_s[ph].tensor, 1024 * 8 * m + 256 * q,
                                  [[1024, 8], [0, 16], [1, 256]])
                    ld2 = em.dma("gpsimd", d_sc[:, 2 * p + ph, :], sap, dsem)
            em.wait("vector", dsem, ld2)
            if self.wr_tgt[p]:
                em.wait("vector", f"GDWR{p}", self.wr_tgt[p])

            e1 = 0
            for nib in range(4):
                e1 = em.op("vector", lambda v, sh=4 * nib, nib=nib, din=din: v.tensor_scalar(
                    d_t[:, nib, :], din, sh, 7, ALU.logical_shift_right, ALU.bitwise_and),
                    inc="SDV")
            em.wait("vector", "SDV", e1)
            e2 = 0
            for nib in range(4):
                e2 = em.op("vector", lambda v, nib=nib: v.tensor_scalar(
                    d_mag[:, nib, :], d_t[:, nib, :], 64, 16128, ALU.mult, ALU.add), inc="SDV")
            e3 = 0
            for nib in range(4):
                msk = 8 << (4 * nib)
                if msk >= 32768:
                    msk -= 65536
                e3 = em.op("vector", lambda v, msk=msk, shl=12 - 4 * nib, nib=nib, din=din:
                           v.tensor_scalar(d_sb2[:, nib, :], din, msk, shl,
                                           ALU.bitwise_and, ALU.logical_shift_left), inc="SDV")
            em.wait("scalar", "SDV", e2)
            em.wait("scalar", "SDV", self.relu_war)
            a1 = 0
            for nib in range(4):
                a1 = em.op("scalar", lambda s2, nib=nib: s2.activation(
                    out=d_t3[:, nib, :], in_=d_mag[:, nib, :].bitcast(BF16),
                    func=AF.Relu, scale=-1.0, bias=1.0), inc="SACT")
            em.wait("vector", "SACT", a1)
            e4 = 0
            for nib in range(4):
                e4 = em.op("vector", lambda v, nib=nib: v.tensor_tensor(
                    d_t[:, nib, :].bitcast(BF16),
                    d_mag[:, nib, :].bitcast(BF16),
                    d_t3[:, nib, :], ALU.subtract), inc="SDV")
            self.relu_war = e4
            em.wait("vector", "SDV", e4)
            em.wait("vector", "SDV", e3)
            e5 = 0
            for nib in range(4):
                e5 = em.op("vector", lambda v, nib=nib: v.tensor_tensor(
                    d_mag[:, nib, :], d_t[:, nib, :], d_sb2[:, nib, :], ALU.bitwise_or), inc="SDV")
            em.wait("vector", "SDV", e5)
            wlast = [0, 0, 0, 0]
            for nib in range(4):
                if kind in ("g", "u"):
                    ph, pn = (nib % 2), (nib // 2)
                    scl = d_sc[:, 2 * p + pn, :]
                    oview = d_out[:, 4 * p + nib, :]
                else:
                    ph, pn = (nib // 2), (nib % 2)
                    scl = d_sc[:, 2 * p + ph, :]
                    oview = d_wdo[:, 2 * p + pn, :].rearrange("a (b c) -> a b c", c=2)[:, :, ph]
                wlast[nib] = em.op("vector", lambda v, nib=nib, oview=oview, scl=scl:
                                   v.tensor_tensor(
                    oview, d_mag[:, nib, :].bitcast(BF16),
                    scl, ALU.mult), inc="SDV")
            self.dv_tgt[p] = wlast[3]
            self.prev_dv = wlast[3]
            wr = 0
            if kind in ("g", "u"):
                dst = sc["wg_dec"] if kind == "g" else sc["wu_dec"]
                for nib in range(4):
                    ph, pn = (nib % 2), (nib // 2)
                    hcx = 2 * m + ph
                    em.wait("scalar", "SDV", wlast[nib])
                    dap = bass.AP(dst.tensor, hcx * 128 * NMLP + 512 * q + 128 * pn,
                                  [[NMLP, 128], [256, 2], [1, 128]])
                    wr = em.dma("scalar", dap, d_out[:, 4 * p + nib, :], f"GDWR{p}")
            else:
                em.wait("scalar", "SDV", wlast[3])
                for pn in range(2):
                    r0 = 256 * m + 128 * pn
                    wr = em.dma("scalar", sc["wd_dec"][r0:r0 + 128, 512 * q:512 * q + 512],
                                d_wdo[:, 2 * p + pn, :], f"GDWR{p}")
            self.wr_tgt[p] = wr

    dec = Decode()

    xn_tm = carve(aren, 0, 8192, BF16, [2, 2048])
    xn_sb = carve(aren, 16384, 8192, BF16, [2, HC, 128])
    qs_sb = carve(aren, 24576, 8192, BF16, [2, D])

    qk_evac_dv = [0, 0, 0, 0]
    xn_pe = [0, 0]
    xn_wr = [0, 0]
    xntm_pe = [0, 0]
    qs_wr = [0, 0]
    xrow_act = 0
    xrow_dv = 0
    var_par = [0, 0]
    tcopy_act = [0, 0, 0, 0]

    stats_par = [stat_sb, bst_sb]

    for st in range(NSTRIP):
        tok0 = st * 128
        pp = st % 2
        stp = stats_par[pp]
        em.wait("sync", "SACT", xrow_act)
        em.wait("sync", "SDV", xrow_dv)
        ld_r = em.dma("sync", xrow_sb, x_tm[tok0:tok0 + 128, :], "LXR")

        em.wait("vector", "LXR", ld_r)
        rs_dv = em.op("vector", lambda v, stp=stp: v.reduce_sum(
            stp[:, 0:1], xrow_sb, axis=AX.X), inc="SDV")
        em.wait("scalar", "LXR", ld_r)
        em.wait("scalar", "SDV", var_par[pp])
        if st == 0:
            em.wait("scalar", "SDV", ones_dv)
        sq1_act = em.op("scalar", lambda s2, stp=stp: s2.activation(
            out=sqs_sb, in_=xrow_sb, func=AF.Square, accum_out=stp[:, 1:2]), inc="SACT")
        xrow_act = sq1_act
        em.wait("vector", "SDV", rs_dv)
        mean_dv = em.op("vector", lambda v, stp=stp: v.tensor_scalar(
            stp[:, 2:3], stp[:, 0:1], 1.0 / D, None, ALU.mult), inc="SDV")
        em.wait("vector", "SDV", mean_dv)
        m2_dv = em.op("vector", lambda v, stp=stp: v.tensor_scalar(
            stp[:, 3:4], stp[:, 2:3], stp[:, 2:3], None, ALU.mult), inc="SDV")
        em.wait("vector", "SDV", m2_dv)
        em.wait("vector", "SACT", sq1_act)
        var_dv = em.op("vector", lambda v, stp=stp: v.scalar_tensor_tensor(
            stp[:, 4:5], stp[:, 1:2], 1.0 / D, stp[:, 3:4],
            ALU.mult, ALU.subtract), inc="SDV")
        var_par[pp] = var_dv
        em.wait("scalar", "SDV", var_dv)
        sd_act = em.op("scalar", lambda s2, stp=stp: s2.activation(
            out=stp[:, 5:6], in_=stp[:, 4:5], func=AF.Sqrt,
            bias=eps_sb[:, 0:1]), inc="SACT")
        em.wait("vector", "SACT", sd_act)
        rc_dv = em.op("vector", lambda v, stp=stp: v.reciprocal(
            stp[:, 3:4], stp[:, 5:6]), inc="SDV")

        em.wait("vector", "SDV", rc_dv)
        em.wait("vector", "SPE", xntm_pe[pp])
        in4 = xrow_sb.rearrange("p (b i q) -> p b q i", b=8, i=128, q=2)
        out4 = xn_tm[:, pp].rearrange("p (b q i) -> p b q i", b=8, q=2, i=128)
        xntm_dv = em.op("vector", lambda v, in4=in4, out4=out4, stp=stp: v.tensor_scalar(
            out4, in4, stp[:, 2:3], stp[:, 3:4], ALU.subtract, ALU.mult), inc="SDV")
        xrow_dv = xntm_dv

        dec.emit(7)
        em.wait("tensor", "SDV", xntm_dv)
        if st == 0:
            em.wait("tensor", "GCAST", 32)
        cp_act = 0
        for hc in range(HC):
            bk = hc % 4
            tp_b = pb[bk].ap().bitcast(BF16)[:, 0:128]
            em.wait("tensor", "SACT", tcopy_act[bk])
            tpe = em.op("tensor", lambda tn, hc=hc, pp=pp, tp_b=tp_b: tn.transpose(
                tp_b, xn_tm[:, pp, 128 * hc:128 * hc + 128], idb_sb), inc="SPE")
            if hc == HC - 1:
                xntm_pe[pp] = tpe
            em.wait("scalar", "SPE", tpe)
            if hc == 0:
                em.wait("scalar", "SPE", xn_pe[pp])
                em.wait("scalar", f"GXW{pp}", xn_wr[pp])
            cp_act = em.op("scalar", lambda s2, hc=hc, pp=pp, tp_b=tp_b: s2.copy(
                xn_sb[:, pp, hc, :], tp_b), inc="SACT")
            tcopy_act[bk] = cp_act
        xn_ready = cp_act

        if st < 32:
            em.wait("gpsimd", "SACT", xn_ready)
            dstap = bass.AP(sc["xn_sc"].tensor, tok0, [[TLOC, 128], [128 * TLOC, HC], [1, 128]])
            xn_wr[pp] = em.dma("gpsimd", dstap, xn_sb[:, pp], f"GXW{pp}")

        em.wait("tensor", "SACT", xn_ready)
        if st == 0:
            em.wait("tensor", "GCAST", wqk_ld)
        for kq in range(2):
            wsb = wq_sb if kq == 0 else wk_sb
            mm = None
            for hc in range(HC):
                for e5 in range(4):
                    if hc == 0:
                        em.wait("tensor", "SDV", qk_evac_dv[e5])
                    last = (hc == HC - 1 and e5 == 3)
                    mm = em.op("tensor", lambda tn, wsb=wsb, hc=hc, e5=e5, pp=pp: tn.matmul(
                        pb[4 + e5][:, :], xn_sb[:, pp, hc, :],
                        wsb[:, hc, 512 * e5:512 * e5 + 512],
                        start=(hc == 0), stop=(hc == HC - 1)),
                        inc="SPE" if last else None)
            pe_qk = mm
            if kq == 1:
                xn_pe[pp] = pe_qk
            em.wait("vector", "SPE", pe_qk)
            em.wait("vector", f"GQW{kq}", qs_wr[kq])
            for e5 in range(4):
                qk_evac_dv[e5] = em.op("vector", lambda v, e5=e5, kq=kq: v.tensor_copy(
                    qs_sb[:, kq, 512 * e5:512 * e5 + 512], pb[4 + e5][:, :]), inc="SDV")
            em.wait("gpsimd", "SDV", qk_evac_dv[3])
            dst = sc["q_sc"] if kq == 0 else sc["k_sc"]
            qs_wr[kq] = em.dma("gpsimd", dst[tok0:tok0 + 128, :], qs_sb[:, kq], f"GQW{kq}")

    a_pe = em.now("SPE")
    dec_wr_all = (em.now("GDWR0"), em.now("GDWR1"))
    if last_phase == 'A':
        return _finish(nc, em, dbg, dbg_outs, sc)

    q_sb = carve(big0, 0, 65536, BF16, [64, 512])
    k_sb = carve(big1, 0, 32768, BF16, [2, 8, 1024])
    sco_sb = carve(big1, 32768, 16384, F32, [2, 2048])
    exp_sb = carve(big1, 49152, 16384, F32, [2, 2048])
    asb = carve(aren, 0, 8192, BF16, [2, 2048])

    blk_ev = [0, 0]
    k_pe = [0, 0]
    a_wr = 0
    bi_g = 0
    es_prev_pe = a_pe

    for es in range(4):
        em.wait("sync", "SPE", es_prev_pe)
        if es == 0:
            em.wait("sync", "GQW0", qs_wr[0])
            em.wait("sync", "GQW1", qs_wr[1])
        ldq = 0
        for g8 in range(8):
            qap = bass.AP(sc["q_sc"].tensor, (8 * g8) * 128 * D + 512 * es,
                          [[D, 128], [D * 128, 8], [1, 512]])
            ldq = em.dma("sync", q_sb[:, 8 * g8:8 * g8 + 8, :], qap, "LQS")
        em.wait("tensor", "LQS", ldq)
        for ep in range(2):
            for fh in range(2):
                seti = bi_g % 2
                bks = [pb[4 * seti + i] for i in range(4)]
                mm = None
                for sc8 in range(8):
                    kp = sc8 % 2
                    em.wait("sync", "SPE", k_pe[kp])
                    kap = bass.AP(sc["k_sc"].tensor, (1024 * sc8) * D + 1024 * fh,
                                  [[D, 128], [D * 128, 8], [1, 1024]])
                    ldk = em.dma("sync", k_sb[:, kp], kap, f"LKC{kp}")
                    em.wait("tensor", f"LKC{kp}", ldk)
                    if sc8 == 0:
                        em.wait("tensor", "SDV", blk_ev[seti])
                    for stt in range(8):
                        for e1 in range(2):
                            for f5 in range(2):
                                last = (stt == 7 and e1 == 1 and f5 == 1)
                                mm = em.op("tensor", lambda tn, stt=stt, e1=e1, f5=f5,
                                           sc8=sc8, ep=ep, kp=kp, bks=bks: tn.matmul(
                                    bks[2 * e1 + f5][:, :],
                                    q_sb[:, 8 * sc8 + stt,
                                         128 * (2 * ep + e1):128 * (2 * ep + e1) + 128],
                                    k_sb[:, kp, stt, 512 * f5:512 * f5 + 512],
                                    start=(sc8 == 0 and stt == 0),
                                    stop=(sc8 == 7 and stt == 7)),
                                    inc="SPE" if last else None)
                    k_pe[kp] = mm
                pe_blk = k_pe[1]
                em.wait("vector", "SPE", pe_blk)
                for e1 in range(2):
                    for f5 in range(2):
                        blk_ev[seti] = em.op("vector", lambda v, e1=e1, f5=f5, fh=fh,
                                             bks=bks: v.tensor_copy(
                            sco_sb[:, e1, 1024 * fh + 512 * f5:1024 * fh + 512 * f5 + 512],
                            bks[2 * e1 + f5][:, :]), inc="SDV")
                bi_g += 1
                dec.emit(8)
            em.wait("vector", "SDV", blk_ev[0])
            em.wait("vector", "SDV", blk_ev[1])
            em.wait("vector", "GAW", a_wr)
            for e1 in range(2):
                mx_dv = em.op("vector", lambda v, e1=e1: v.reduce_max(
                    bst_sb[:, 0:1], sco_sb[:, e1, :], axis=AX.X), inc="SDV")
                em.wait("vector", "SDV", mx_dv)
                sm1_dv = em.op("vector", lambda v: v.tensor_scalar(
                    bst_sb[:, 1:2], bst_sb[:, 0:1], -0.125, None, ALU.mult), inc="SDV")
                em.wait("scalar", "SDV", sm1_dv)
                ex_act = em.op("scalar", lambda s2, e1=e1: s2.activation(
                    out=exp_sb[:, e1, :], in_=sco_sb[:, e1, :], func=AF.Exp,
                    bias=bst_sb[:, 1:2], scale=0.125, accum_out=bst_sb[:, 2:3]), inc="SACT")
                em.wait("vector", "SACT", ex_act)
                rc2_dv = em.op("vector", lambda v: v.reciprocal(
                    bst_sb[:, 3:4], bst_sb[:, 2:3]), inc="SDV")
                em.wait("vector", "SDV", rc2_dv)
                a_dv = em.op("vector", lambda v, e1=e1: v.tensor_scalar(
                    asb[:, e1, :], exp_sb[:, e1, :], bst_sb[:, 3:4], None, ALU.mult), inc="SDV")
                em.wait("gpsimd", "SDV", a_dv)
                erow = 512 * es + 256 * ep + 128 * e1
                a_wr = em.dma("gpsimd", sc["a_sc"][erow:erow + 128, :], asb[:, e1, :], "GAW")
        es_prev_pe = em.now("SPE")

    b_pe = em.now("SPE")
    if last_phase == 'B':
        dec.emit(999)
        return _finish(nc, em, dbg, dbg_outs, sc)

    wv_sb = carve(big0, 0, 65536, BF16, [HC, D])
    em.wait("gpsimd", "SPE", b_pe)
    wv_ld = em.dma("gpsimd", wv_sb, wrows(wv_i), "GCAST")
    at_sb = carve(big1, 0, 65536, BF16, [HC, 2048])
    vtb_sb = carve(aren, 0, 16384, BF16, [HC, 512])
    xnc = carve(aren, 16384, 16384, BF16, [HC, 512])
    arow = ist_sb.rearrange("p a b -> p (a b)")
    c2st = ist_sb

    tb_dv = [0, 0]
    at_ev = 0
    for ec in range(HC):
        if ec == 0:
            em.wait("sync", "GAW", a_wr)
            em.wait("sync", "SPE", b_pe)
        else:
            em.wait("sync", "SDV", at_ev)
        lda = em.dma("sync", arow, sc["a_sc"][128 * ec:128 * ec + 128, :], "LAR")
        em.wait("tensor", "LAR", lda)
        for fc in range(HC):
            tp_b = pb[4 + fc % 2].ap().bitcast(BF16)[:, 0:128]
            em.wait("tensor", "SDV", tb_dv[fc % 2])
            tpe = em.op("tensor", lambda tn, fc=fc, tp_b=tp_b: tn.transpose(
                tp_b, arow[:, 128 * fc:128 * fc + 128], idb_sb), inc="SPE")
            em.wait("vector", "SPE", tpe)
            tb_dv[fc % 2] = em.op("vector", lambda v, fc=fc, ec=ec, tp_b=tp_b: v.tensor_copy(
                at_sb[:, fc, 128 * ec:128 * ec + 128], tp_b), inc="SDV")
        at_ev = tb_dv[(HC - 1) % 2]
    at_done_dv = em.now("SDV")

    vban_dv = [0, 0, 0, 0]
    c2_wr = [0, 0]
    ob_dv = [0, 0, 0, 0]
    xnc_pe = 0
    for sb8 in range(8):
        em.wait("sync", "SPE", xnc_pe if sb8 > 0 else b_pe)
        if sb8 == 0:
            em.wait("sync", "SDV", at_done_dv)
        xap = bass.AP(sc["xn_sc"].tensor, 512 * sb8, [[TLOC, 128], [128 * TLOC, HC], [1, 512]])
        ldxn = em.dma("sync", xnc, xap, "LXC")
        em.wait("tensor", "LXC", ldxn)
        if sb8 == 0:
            em.wait("tensor", "GCAST", wv_ld)
            em.wait("tensor", "SDV", at_done_dv)
        for fc in range(HC):
            bk = fc % 4
            mm = None
            for hc in range(HC):
                if hc == 0:
                    em.wait("tensor", "SDV", vban_dv[bk])
                mm = em.op("tensor", lambda tn, fc=fc, hc=hc, bk=bk: tn.matmul(
                    pb[bk][:, :], wv_sb[:, hc, 128 * fc:128 * fc + 128],
                    xnc[:, hc, :], start=(hc == 0), stop=(hc == HC - 1)),
                    inc="SPE" if hc == HC - 1 else None)
            vpe = mm
            if fc == HC - 1:
                xnc_pe = vpe
            em.wait("vector", "SPE", vpe)
            vban_dv[bk] = em.op("vector", lambda v, fc=fc, bk=bk: v.tensor_copy(
                vtb_sb[:, fc, :], pb[bk][:, :]), inc="SDV")
        vt_dv = em.now("SDV")
        em.wait("tensor", "SDV", vt_dv)
        for ec in range(HC):
            bk = 4 + ec % 4
            mm = None
            for fc in range(HC):
                if fc == 0:
                    em.wait("tensor", "SDV", ob_dv[ec % 4])
                mm = em.op("tensor", lambda tn, ec=ec, fc=fc, bk=bk: tn.matmul(
                    pb[bk][:, :], at_sb[:, fc, 128 * ec:128 * ec + 128],
                    vtb_sb[:, fc, :], start=(fc == 0), stop=(fc == HC - 1)),
                    inc="SPE" if fc == HC - 1 else None)
            mpe = mm
            em.wait("vector", "SPE", mpe)
            em.wait("vector", f"GC2W{ec % 2}", c2_wr[ec % 2])
            ob_dv[ec % 4] = em.op("vector", lambda v, ec=ec, bk=bk: v.tensor_copy(
                c2st[:, ec % 2, :], pb[bk][:, :]), inc="SDV")
            em.wait("gpsimd", "SDV", ob_dv[ec % 4])
            off = 512 * sb8
            c2_wr[ec % 2] = em.dma("gpsimd", sc["attnT_sc"][ec, :, off:off + 512],
                                   c2st[:, ec % 2, :], f"GC2W{ec % 2}")
        dec.emit(12)

    dec.emit(999)
    c2_pe = em.now("SPE")
    c2_wr_all = (em.now("GC2W0"), em.now("GC2W1"))
    if last_phase == 'C':
        return _finish(nc, em, dbg, dbg_outs, sc)

    wo_sb = carve(big0, 0, 65536, BF16, [HC, D])
    em.wait("gpsimd", "SPE", c2_pe)
    wo_ld = em.dma("gpsimd", wo_sb, wrows(wo_i), "GCAST")
    atn_sb = carve(big1, 0, 32768, BF16, [2, HC, 512])
    af_st = carve(aren, 0, 8192, F32, [4, 512])

    atn_pe = [0, 0]
    c3ob_dv = [0, 0, 0, 0, 0, 0, 0, 0]
    af_wr = [0, 0]
    for s5 in range(8):
        ap_ = s5 % 2
        em.wait("sync", "SPE", atn_pe[ap_])
        if s5 == 0:
            em.wait("sync", "GC2W0", c2_wr_all[0])
            em.wait("sync", "GC2W1", c2_wr_all[1])
            em.wait("sync", "SPE", c2_pe)
        aap = bass.AP(sc["attnT_sc"].tensor, 512 * s5,
                      [[TLOC, 128], [128 * TLOC, HC], [1, 512]])
        lda = em.dma("sync", atn_sb[:, ap_], aap, f"LAT{ap_}")
        em.wait("tensor", f"LAT{ap_}", lda)
        if s5 == 0:
            em.wait("tensor", "GCAST", wo_ld)
        dec.emit(10)
        bset = 4 * (s5 % 2)
        for st1 in range(4):
            mm = None
            for ec in range(HC):
                for g5 in range(4):
                    if ec == 0:
                        em.wait("tensor", "SDV", c3ob_dv[bset + g5])
                    last = (ec == HC - 1 and g5 == 3)
                    mm = em.op("tensor", lambda tn, ap_=ap_, st1=st1, ec=ec, g5=g5,
                               bset=bset: tn.matmul(
                        pb[bset + g5][:, :],
                        atn_sb[:, ap_, ec, 128 * st1:128 * st1 + 128],
                        wo_sb[:, ec, 512 * g5:512 * g5 + 512],
                        start=(ec == 0), stop=(ec == HC - 1)),
                        inc="SPE" if last else None)
            mpe = mm
            atn_pe[ap_] = mpe
            em.wait("vector", "SPE", mpe)
            em.wait("vector", f"GAFW{s5 % 2}", af_wr[s5 % 2])
            for g5 in range(4):
                c3ob_dv[bset + g5] = em.op("vector", lambda v, g5=g5, bset=bset: v.tensor_copy(
                    af_st[:, g5, :], pb[bset + g5][:, :]), inc="SDV")
            em.wait("gpsimd", "SDV", c3ob_dv[bset + 3])
            tok0 = 512 * s5 + 128 * st1
            af_wr[s5 % 2] = em.dma("gpsimd", sc["afin_sc"][tok0:tok0 + 128, :],
                                   af_st.rearrange("p a b -> p (a b)"), f"GAFW{s5 % 2}")

    dec.emit(999)
    c3_pe = em.now("SPE")
    c3_wr_all = (em.now("GAFW0"), em.now("GAFW1"))
    if last_phase == 'C3':
        return _finish(nc, em, dbg, dbg_outs, sc)

    xnh_sb = carve(big0, 0, 65536, BF16, [HC, 2048])
    wgu_sb = carve(big1, 0, 32768, BF16, [2, 2, HC, 256])

    gel_dv = [0, 0, 0, 0]
    sig_war = [0, 0, 0, 0]
    ist_wr = [0, 0, 0, 0]
    wgu_pe = [0, 0]
    d_first = True
    for dh in range(2):
        em.wait("sync", "SPE", c3_pe if dh == 0 else em.now("SPE"))
        ldh = 0
        for hc in range(HC):
            xap = bass.AP(sc["xn_sc"].tensor, hc * 128 * TLOC + 2048 * dh,
                          [[TLOC, 128], [1, 2048]])
            ldh = em.dma("sync", xnh_sb[:, hc, :], xap, "LXH")
        for ntp in range(64):
            wp = ntp % 2
            em.wait("sync", "SPE", wgu_pe[wp])
            if d_first:
                em.wait("sync", "GDWR0", dec_wr_all[0])
                em.wait("sync", "GDWR1", dec_wr_all[1])
                d_first = False
            gap = bass.AP(sc["wg_dec"].tensor, 256 * ntp, [[NMLP, 128], [128 * NMLP, HC], [1, 256]])
            uap = bass.AP(sc["wu_dec"].tensor, 256 * ntp, [[NMLP, 128], [128 * NMLP, HC], [1, 256]])
            em.dma("sync", wgu_sb[:, 0, wp], gap, f"LWG{wp}")
            ldw = em.dma("sync", wgu_sb[:, 1, wp], uap, f"LWG{wp}")
            for nt1 in range(2):
                nt = 2 * ntp + nt1
                em.wait("tensor", f"LWG{wp}", ldw)
                if ntp == 0 and nt1 == 0:
                    em.wait("tensor", "LXH", ldh)
                gpe = upe = None
                for gu in range(2):
                    mm = None
                    for hc in range(HC):
                        for t5 in range(4):
                            if hc == 0:
                                em.wait("tensor", "SDV", gel_dv[t5])
                            last = (hc == HC - 1 and t5 == 3)
                            mm = em.op("tensor", lambda tn, gu=gu, hc=hc, t5=t5, wp=wp,
                                       nt1=nt1: tn.matmul(
                                pb[4 * gu + t5][:, :],
                                wgu_sb[:, gu, wp, hc, 128 * nt1:128 * nt1 + 128],
                                xnh_sb[:, hc, 512 * t5:512 * t5 + 512],
                                start=(hc == 0), stop=(hc == HC - 1)),
                                inc="SPE" if last else None)
                    if gu == 0:
                        gpe = mm
                    else:
                        upe = mm
                wgu_pe[wp] = upe
                for t5 in range(4):
                    gb, ub = pb[t5], pb[4 + t5]
                    gslot = gel_sb[:, t5, :]
                    qslot = sqs_sb[:, 512 * t5:512 * t5 + 512]
                    em.wait("scalar", "SPE", gpe)
                    em.wait("scalar", "SDV", gel_dv[t5])
                    s2m = em.op("scalar", lambda s2_, gb=gb, gslot=gslot: s2_.activation(
                        out=gslot, in_=gb[:, :], func=AF.Square), inc="SACT")
                    em.wait("vector", "SACT", s2m)
                    em.wait("vector", f"GIW{t5}", ist_wr[t5])
                    uum = em.op("vector", lambda v, gslot=gslot: v.tensor_scalar(
                        gslot, gslot, GELU_B, 1.0, ALU.mult, ALU.add), inc="SDV")
                    em.wait("vector", "SDV", uum)
                    w1m = em.op("vector", lambda v, gb=gb, gslot=gslot: v.tensor_tensor(
                        gslot, gslot, gb[:, :], ALU.mult), inc="SDV")
                    em.wait("scalar", "SDV", w1m)
                    em.wait("scalar", "SDV", sig_war[t5])
                    sgm = em.op("scalar", lambda s2, gslot=gslot, qslot=qslot: s2.activation(
                        out=qslot, in_=gslot, func=AF.Sigmoid, scale=GELU_A), inc="SACT")
                    em.wait("vector", "SACT", sgm)
                    w2m = em.op("vector", lambda v, gb=gb, gslot=gslot, qslot=qslot:
                                v.tensor_tensor(gslot, qslot, gb[:, :], ALU.mult), inc="SDV")
                    sig_war[t5] = w2m
                    em.wait("vector", "SDV", w2m)
                    em.wait("vector", "SPE", upe)
                    gel_dv[t5] = em.op("vector", lambda v, ub=ub, t5=t5, gslot=gslot:
                                       v.tensor_tensor(
                        ist_sb[:, t5, :], gslot, ub[:, :], ALU.mult), inc="SDV")
                    em.wait("gpsimd", "SDV", gel_dv[t5])
                    off = 2048 * dh + 512 * t5
                    ist_wr[t5] = em.dma("gpsimd", sc["inter_sc"][nt, :, off:off + 512],
                                        ist_sb[:, t5, :], f"GIW{t5}")

    d_pe = em.now("SPE")
    d_wr_all = tuple(em.now(f"GIW{i}") for i in range(4))
    if last_phase == 'D':
        return _finish(nc, em, dbg, dbg_outs, sc)

    wdt_lo = carve(big0, 0, 65536, BF16, [64, 512])
    wdt_hi = carve(big1, 0, 65536, BF16, [64, 512])

    def wdt(nt):
        return wdt_lo[:, nt, :] if nt < 64 else wdt_hi[:, nt - 64, :]

    ich_sb = carve(aren, 0, 32768, BF16, [2, HC, 512])
    xs_sb = xrow_sb.rearrange("p (a b) -> p a b", a=4)
    as_sb = sqs_sb.rearrange("p (a b) -> p a b", a=4)

    e_first = True
    ich_pe = [0, 0]
    eob_dv = [0, 0, 0, 0, 0, 0, 0, 0]
    o_wr = [0, 0]
    xa_dv = 0
    wdt_pe = 0
    for hp in range(4):
        em.wait("sync", "SPE", wdt_pe)
        if e_first:
            em.wait("sync", "SPE", d_pe)
            for i4 in range(4):
                em.wait("sync", f"GIW{i4}", d_wr_all[i4])
            e_first = False
        ldwd = 0
        for i in range(16):
            wap = bass.AP(sc["wd_dec"].tensor, (1024 * i) * D + 512 * hp,
                          [[D, 128], [D * 128, 8], [1, 512]])
            ldwd = em.dma("sync",
                          (wdt_lo[:, 8 * i:8 * i + 8, :] if i < 8
                           else wdt_hi[:, 8 * (i - 8):8 * (i - 8) + 8, :]),
                          wap, "LWD")
        for t5 in range(8):
            bset = 4 * (t5 % 2)
            em.wait("sync", "SDV", xa_dv)
            if hp == 0 and t5 == 0:
                em.wait("sync", "GAFW0", c3_wr_all[0])
                em.wait("sync", "GAFW1", c3_wr_all[1])
            xap = bass.AP(x_tm.tensor, (512 * t5) * D + 512 * hp,
                          [[D, 128], [D * 128, 4], [1, 512]])
            aap = bass.AP(sc["afin_sc"].tensor, (512 * t5) * D + 512 * hp,
                          [[D, 128], [D * 128, 4], [1, 512]])
            em.dma("sync", xs_sb, xap, "LXA")
            ldxa = em.dma("sync", as_sb, aap, "LXA")
            mm = None
            for e8 in range(8):
                ip = e8 % 2
                em.wait("sync", "SPE", ich_pe[ip])
                iap = bass.AP(sc["inter_sc"].tensor, (16 * e8) * 128 * TLOC + 512 * t5,
                              [[TLOC, 128], [128 * TLOC, HC], [1, 512]])
                ldi = em.dma("sync", ich_sb[:, ip], iap, f"LIC{ip}")
                em.wait("tensor", f"LIC{ip}", ldi)
                if e8 == 0:
                    em.wait("tensor", "LWD", ldwd)
                    for t1 in range(4):
                        em.wait("tensor", "SDV", eob_dv[bset + t1])
                for ntl in range(HC):
                    nt = 16 * e8 + ntl
                    for t1 in range(4):
                        last = (ntl == HC - 1 and t1 == 3)
                        mm = em.op("tensor", lambda tn, ip=ip, ntl=ntl, nt=nt, t1=t1,
                                   bset=bset: tn.matmul(
                            pb[bset + t1][:, :],
                            ich_sb[:, ip, ntl, 128 * t1:128 * t1 + 128],
                            wdt(nt), start=(nt == 0), stop=(nt == 127)),
                            inc="SPE" if last else None)
                ich_pe[ip] = mm
            mpe = ich_pe[1]
            if t5 == 7:
                wdt_pe = mpe
            em.wait("vector", "SPE", mpe)
            em.wait("vector", "LXA", ldxa)
            for t1 in range(4):
                osl = 2 + t1 % 2
                em.wait("vector", f"GOW{t1 % 2}", o_wr[t1 % 2])
                r1 = em.op("vector", lambda v, t1=t1, bset=bset: v.tensor_tensor(
                    gel_sb[:, 0, :], as_sb[:, t1, :], pb[bset + t1][:, :], ALU.add), inc="SDV")
                em.wait("vector", "SDV", r1)
                eob_dv[bset + t1] = em.op("vector", lambda v, t1=t1, osl=osl: v.tensor_tensor(
                    gel_sb[:, osl, :], gel_sb[:, 0, :], xs_sb[:, t1, :], ALU.add), inc="SDV")
                em.wait("gpsimd", "SDV", eob_dv[bset + t1])
                tok0 = 512 * t5 + 128 * t1
                o_wr[t1 % 2] = em.dma("gpsimd",
                                      out_loc[tok0:tok0 + 128, 512 * hp:512 * hp + 512],
                                      gel_sb[:, osl, :], f"GOW{t1 % 2}")
            xa_dv = eob_dv[bset + 3]

    return _finish(nc, em, dbg, dbg_outs, sc)


def _finish(nc, em, dbg, dbg_outs, sc):
    for name in dbg:
        em.dma("gpsimd", dbg_outs[name], sc[name], "GDBG")
    for s_ in em.dma_sems:
        eng = "gpsimd" if s_.startswith("G") else "sync"
        em.wait(eng, s_, em.now(s_))
    with nc.Block() as block:
        @block.sync
        def _(e):
            for f in em.streams["sync"]:
                f(e)

        @block.gpsimd
        def _(e):
            for f in em.streams["gpsimd"]:
                f(e)

        @block.scalar
        def _(e):
            for f in em.streams["scalar"]:
                f(e)

        @block.vector
        def _(e):
            for f in em.streams["vector"]:
                f(e)

        @block.tensor
        def _(e):
            for f in em.streams["tensor"]:
                f(e)
    counts = {k: len(v) for k, v in em.streams.items()}
    return nc, counts


def host_inputs(inp):
    import ml_dtypes
    perm = _perm_h()
    x = np.asarray(inp["L0_attn_x"], dtype=np.float32)
    wq = np.asarray(inp["L0_attn_Wq"], dtype=np.float32)
    wk = np.asarray(inp["L0_attn_Wk"], dtype=np.float32)
    wv = np.asarray(inp["L0_attn_Wv"], dtype=np.float32)
    wo = np.asarray(inp["L0_attn_Wo"], dtype=np.float32)

    def pack_t(p_int32):
        b = np.ascontiguousarray(p_int32).view(np.uint8).reshape(
            p_int32.shape[0], p_int32.shape[1], 4)[:, :, 0]
        bt = np.ascontiguousarray(b.T)
        return bt.view("<i2")

    pg = pack_t(np.asarray(inp["L0_ffn_gate_W_packed"]))
    pu = pack_t(np.asarray(inp["L0_ffn_up_W_packed"]))
    pd = pack_t(np.asarray(inp["L0_ffn_down_W_packed"]))
    sgt = np.ascontiguousarray(np.asarray(inp["L0_ffn_gate_scales"], dtype=np.float32).T)
    sut = np.ascontiguousarray(np.asarray(inp["L0_ffn_up_scales"], dtype=np.float32).T)
    sdt = np.ascontiguousarray(np.asarray(inp["L0_ffn_down_scales"], dtype=np.float32).T)
    sg0 = np.ascontiguousarray(sgt[:, 0::2])
    sg1 = np.ascontiguousarray(sgt[:, 1::2])
    su0 = np.ascontiguousarray(sut[:, 0::2])
    su1 = np.ascontiguousarray(sut[:, 1::2])
    sd0 = np.ascontiguousarray(sdt[:, 0::2])
    sd1 = np.ascontiguousarray(sdt[:, 1::2])
    shared = {
        "wq": np.ascontiguousarray(wq[perm]),
        "wk": np.ascontiguousarray(wk[perm]),
        "wv": np.ascontiguousarray(wv[perm]),
        "wo": np.ascontiguousarray(wo),
        "idb": np.eye(128, dtype=np.float32).astype(ml_dtypes.bfloat16),
    }
    maps = []
    for c in range(8):
        b, h = c // 2, c % 2
        lf = np.concatenate([np.arange(h * 4096, (h + 1) * 4096),
                             np.arange((1 - h) * 4096, (2 - h) * 4096)])
        xb = x[b][lf]
        m = dict(shared)
        m["x_tm"] = np.ascontiguousarray(xb)
        m["pg16"] = pg
        m["pu16"] = pu
        m["pd16"] = pd
        m["sg0"] = sg0
        m["sg1"] = sg1
        m["su0"] = su0
        m["su1"] = su1
        m["sd0"] = sd0
        m["sd1"] = sd1
        maps.append(m)
    return maps


def kernel(**inputs):
    nc, _ = build_program(dbg=DBG)
    maps = host_inputs(inputs)
    res = run_bass_kernel_spmd(nc, maps, core_ids=list(range(8)))
    out = np.empty((B_, S, D), dtype=np.float32)
    for c in range(8):
        b, h = c // 2, c % 2
        out[b, h * 4096:(h + 1) * 4096, :] = res.results[c]["out_loc"]
    return out


if __name__ == "__main__":
    nc, counts = build_program()
    print("instruction closures per engine:", counts)


# version 13
# speedup vs baseline: 1.0410x; 1.0263x over previous
import numpy as np
import concourse.bass as bass
import concourse.mybir as mybir
from concourse.bass_utils import run_bass_kernel_spmd

F32 = mybir.dt.float32
BF16 = mybir.dt.bfloat16
I16 = mybir.dt.int16
U8 = mybir.dt.uint8
AF = mybir.ActivationFunctionType
ALU = mybir.AluOpType
AX = mybir.AxisListType

B_, S, D = 4, 8192, 2048
NMLP = 16384
TLOC = 4096
HC = 16
NSTRIP = 64
GELU_A = 1.5957691216056908
GELU_B = 0.044715

DBG = []


def _perm_h():
    p = []
    for m in range(8):
        p += [256 * m + 2 * i for i in range(128)]
        p += [256 * m + 2 * i + 1 for i in range(128)]
    return np.array(p, dtype=np.int64)


class Emit:
    def __init__(self, nc):
        self.nc = nc
        self.streams = {k: [] for k in ("sync", "gpsimd", "scalar", "vector", "tensor")}
        self.sems = {}
        self.cnt = {}

    def sem(self, name):
        h = self.nc.semaphore(name).__enter__()
        self.sems[name] = h
        self.cnt[name] = 0

    def add(self, eng, fn):
        self.streams[eng].append(fn)

    def dma(self, eng, out_ap, in_ap, sem_name):
        h = self.sems[sem_name]
        self.cnt[sem_name] += 16
        tgt = self.cnt[sem_name]

        def _f(e, out_ap=out_ap, in_ap=in_ap, h=h):
            e.dma_start(out=out_ap, in_=in_ap).then_inc(h, 16)
        self.add(eng, _f)
        return tgt

    def op(self, eng, builder, inc=None):
        tgt, h = None, None
        if inc is not None:
            self.cnt[inc] += 1
            tgt = self.cnt[inc]
            h = self.sems[inc]

        def _f(e, builder=builder, h=h):
            ins = builder(e)
            if h is not None:
                ins.then_inc(h, 1)
        self.add(eng, _f)
        return tgt

    def wait(self, eng, sem_name, val):
        if not val:
            return
        h = self.sems[sem_name]

        def _f(e, h=h, val=val):
            e.wait_ge(h, val)
        self.add(eng, _f)

    def now(self, sem_name):
        return self.cnt[sem_name]


def build_program(dbg=(), last_phase='E'):
    nc = bass.Bass("TRN2", target_bir_lowering=False)
    em = Emit(nc)
    DMA_SEMS = ["LXT0", "LXT1", "LXR", "LQS", "LKC0", "LKC1", "LAR", "LXC",
                "LAT0", "LAT1", "LXH", "LWG0", "LWG1", "LWD", "LIC0", "LIC1", "LXA",
                "GCAST", "GDEC0", "GDEC1", "GDWR0", "GDWR1", "GXW0", "GXW1",
                "GQW0", "GQW1", "GAW", "GC2W0", "GC2W1", "GAFW0", "GAFW1",
                "GIW0", "GIW1", "GIW2", "GIW3", "GOW0", "GOW1", "GDBG", "GAGR"]
    for s_ in ["SPE", "SDV", "SACT"] + DMA_SEMS:
        em.sem(s_)
    em.dma_sems = DMA_SEMS

    x_tm = nc.dram_tensor("x_tm", [S, D], F32, kind="ExternalInput").ap()
    wq_i = nc.dram_tensor("wq", [D, D], F32, kind="ExternalInput").ap()
    wk_i = nc.dram_tensor("wk", [D, D], F32, kind="ExternalInput").ap()
    wv_i = nc.dram_tensor("wv", [D, D], F32, kind="ExternalInput").ap()
    wo_i = nc.dram_tensor("wo", [D, D], F32, kind="ExternalInput").ap()
    pg16 = nc.dram_tensor("pg16", [1024, 8192], I16, kind="ExternalInput").ap()
    pu16 = nc.dram_tensor("pu16", [1024, 8192], I16, kind="ExternalInput").ap()
    pd16 = nc.dram_tensor("pd16", [8192, 1024], I16, kind="ExternalInput").ap()
    sg_s = [nc.dram_tensor(f"sg{p}", [64, 8192], F32, kind="ExternalInput").ap() for p in range(2)]
    su_s = [nc.dram_tensor(f"su{p}", [64, 8192], F32, kind="ExternalInput").ap() for p in range(2)]
    sd_s = [nc.dram_tensor(f"sd{p}", [512, 1024], F32, kind="ExternalInput").ap() for p in range(2)]
    idb_i = nc.dram_tensor("idb", [128, 128], BF16, kind="ExternalInput").ap()
    out_loc = nc.dram_tensor("out_loc", [TLOC, D], F32, kind="ExternalOutput").ap()

    sc = {
        "xn_sc": nc.dram_tensor("xn_sc", [HC, 128, TLOC], BF16).ap(),
        "q_sc": nc.dram_tensor("q_sc", [S, D], BF16).ap(),
        "k_sc": nc.dram_tensor("k_sc", [S, D], BF16).ap(),
        "a_sc": nc.dram_tensor("a_sc", [D, D], BF16).ap(),
        "attnT_sc": nc.dram_tensor("attnT_sc", [HC, 128, TLOC], BF16).ap(),
        "afin_sc": nc.dram_tensor("afin_sc", [TLOC, D], F32).ap(),
        "wg_dec": nc.dram_tensor("wg_dec", [HC, 128, NMLP], BF16).ap(),
        "wu_dec": nc.dram_tensor("wu_dec", [HC, 128, NMLP], BF16).ap(),
        "wd_dec": nc.dram_tensor("wd_dec", [NMLP, D], BF16).ap(),
        "inter_sc": nc.dram_tensor("inter_sc", [128, 128, TLOC], BF16).ap(),
    }
    dbg_outs = {}
    for name in dbg:
        t = sc[name].tensor
        dbg_outs[name] = nc.dram_tensor("dbg_" + name, list(t.shape), t.dtype,
                                        kind="ExternalOutput").ap()

    big0 = nc.alloc_sbuf_tensor("big0", [128, 65536], U8).ap()
    big1 = nc.alloc_sbuf_tensor("big1", [128, 65536], U8).ap()
    aren = nc.alloc_sbuf_tensor("aren", [128, 32768], U8).ap()

    def carve(base, off, nbytes, dtype, shape):
        ap = base[:, off:off + nbytes].bitcast(dtype)
        if len(shape) == 1:
            return ap
        spec = "p (" + " ".join(f"d{i}" for i in range(len(shape))) + ") -> p " + \
               " ".join(f"d{i}" for i in range(len(shape)))
        return ap.rearrange(spec, **{f"d{i}": shape[i] for i in range(len(shape))})

    xrow_sb = nc.alloc_sbuf_tensor("xrow", [128, D], F32).ap()
    sqs_sb = nc.alloc_sbuf_tensor("sqs", [128, D], F32).ap()
    stat_sb = nc.alloc_sbuf_tensor("stat", [128, 8], F32).ap()
    bst_sb = nc.alloc_sbuf_tensor("bst", [128, 8], F32).ap()
    ones_sb = nc.alloc_sbuf_tensor("ones1", [1, 128], F32).ap()
    eps_sb = nc.alloc_sbuf_tensor("eps1", [128, 1], F32).ap()
    idb_sb = nc.alloc_sbuf_tensor("idb_sb", [128, 128], BF16).ap()
    d_in = nc.alloc_sbuf_tensor("d_in", [128, 2, 256], I16).ap()
    d_t = nc.alloc_sbuf_tensor("d_t", [128, 4, 256], I16).ap()
    d_mag = nc.alloc_sbuf_tensor("d_mag", [128, 4, 256], I16).ap()
    d_sb2 = nc.alloc_sbuf_tensor("d_sb2", [128, 4, 256], I16).ap()
    d_t3 = nc.alloc_sbuf_tensor("d_t3", [128, 4, 256], BF16).ap()
    d_out = nc.alloc_sbuf_tensor("d_out", [128, 8, 256], BF16).ap()
    d_wdo = nc.alloc_sbuf_tensor("d_wdo", [128, 4, 512], BF16).ap()
    ist_sb = nc.alloc_sbuf_tensor("ist", [128, 4, 512], BF16).ap()
    gel_sb = nc.alloc_sbuf_tensor("gel", [128, 4, 512], F32).ap()

    d_sc = carve(aren, 8192, 4096, F32, [4, 256])
    pb = [nc.alloc_psum_tensor(f"pb{i}", [128, 512], F32) for i in range(8)]

    em.op("vector", lambda v: v.memset(ones_sb, 1.0))
    ones_dv = em.op("vector", lambda v: v.memset(eps_sb, 1e-5), inc="SDV")
    em.dma("gpsimd", idb_sb, idb_i, "GCAST")
    wq_sb = carve(big0, 0, 65536, BF16, [HC, D])
    wk_sb = carve(big1, 0, 65536, BF16, [HC, D])

    def wrows(w_ap):
        return bass.AP(w_ap.tensor, 0, [[D, 128], [128 * D, HC], [1, D]])
    em.dma("gpsimd", wq_sb, wrows(wq_i), "GCAST")
    wqk_ld = em.dma("gpsimd", wk_sb, wrows(wk_i), "GCAST")

    class Decode:

        def __init__(self):
            self.jobs = []
            for q in range(32):
                for m in range(8):
                    self.jobs.append(("g", m, q))
            for q in range(32):
                for m in range(8):
                    self.jobs.append(("u", m, q))
            for q in range(4):
                for m in range(64):
                    self.jobs.append(("d", m, q))
            self.k = 0
            self.wr_tgt = [0, 0]
            self.dv_tgt = [0, 0]
            self.prev_dv = 0
            self.relu_war = 0

        def done(self):
            return self.k >= len(self.jobs)

        def emit(self, n):
            for _ in range(n):
                if self.done():
                    return
                self._one(self.k)
                self.k += 1

        def _one(self, k):
            kind, m, q = self.jobs[k]
            p = k % 2
            din = d_in[:, p, :]
            if self.dv_tgt[p]:
                em.wait("gpsimd", "SDV", self.dv_tgt[p])
            dsem = f"GDEC{p}"
            if kind in ("g", "u"):
                src = pg16 if kind == "g" else pu16
                scs = sg_s if kind == "g" else su_s
                em.dma("gpsimd", din, src[128 * m:128 * m + 128, 256 * q:256 * q + 256], dsem)
                ld2 = 0
                for pn in range(2):
                    sap = bass.AP(scs[pn].tensor, 8192 * 8 * m + 256 * q,
                                  [[8192, 8], [0, 16], [1, 256]])
                    ld2 = em.dma("gpsimd", d_sc[:, 2 * p + pn, :], sap, dsem)
            else:
                em.dma("gpsimd", din, pd16[128 * m:128 * m + 128, 256 * q:256 * q + 256], dsem)
                ld2 = 0
                for ph in range(2):
                    sap = bass.AP(sd_s[ph].tensor, 1024 * 8 * m + 256 * q,
                                  [[1024, 8], [0, 16], [1, 256]])
                    ld2 = em.dma("gpsimd", d_sc[:, 2 * p + ph, :], sap, dsem)
            em.wait("vector", dsem, ld2)
            if self.wr_tgt[p]:
                em.wait("vector", f"GDWR{p}", self.wr_tgt[p])

            e1 = 0
            for nib in range(4):
                e1 = em.op("vector", lambda v, sh=4 * nib, nib=nib, din=din: v.tensor_scalar(
                    d_t[:, nib, :], din, sh, 7, ALU.logical_shift_right, ALU.bitwise_and),
                    inc="SDV")
            em.wait("vector", "SDV", e1)
            e2 = 0
            for nib in range(4):
                e2 = em.op("vector", lambda v, nib=nib: v.tensor_scalar(
                    d_mag[:, nib, :], d_t[:, nib, :], 64, 16128, ALU.mult, ALU.add), inc="SDV")
            e3 = 0
            for nib in range(4):
                msk = 8 << (4 * nib)
                if msk >= 32768:
                    msk -= 65536
                e3 = em.op("vector", lambda v, msk=msk, shl=12 - 4 * nib, nib=nib, din=din:
                           v.tensor_scalar(d_sb2[:, nib, :], din, msk, shl,
                                           ALU.bitwise_and, ALU.logical_shift_left), inc="SDV")
            em.wait("scalar", "SDV", e2)
            em.wait("scalar", "SDV", self.relu_war)
            a1 = 0
            for nib in range(4):
                a1 = em.op("scalar", lambda s2, nib=nib: s2.activation(
                    out=d_t3[:, nib, :], in_=d_mag[:, nib, :].bitcast(BF16),
                    func=AF.Relu, scale=-1.0, bias=1.0), inc="SACT")
            em.wait("vector", "SACT", a1)
            e4 = 0
            for nib in range(4):
                e4 = em.op("vector", lambda v, nib=nib: v.tensor_tensor(
                    d_t[:, nib, :].bitcast(BF16),
                    d_mag[:, nib, :].bitcast(BF16),
                    d_t3[:, nib, :], ALU.subtract), inc="SDV")
            self.relu_war = e4
            em.wait("vector", "SDV", e4)
            em.wait("vector", "SDV", e3)
            e5 = 0
            for nib in range(4):
                e5 = em.op("vector", lambda v, nib=nib: v.tensor_tensor(
                    d_mag[:, nib, :], d_t[:, nib, :], d_sb2[:, nib, :], ALU.bitwise_or), inc="SDV")
            em.wait("vector", "SDV", e5)
            wlast = [0, 0, 0, 0]
            for nib in range(4):
                if kind in ("g", "u"):
                    ph, pn = (nib % 2), (nib // 2)
                    scl = d_sc[:, 2 * p + pn, :]
                    oview = d_out[:, 4 * p + nib, :]
                else:
                    ph, pn = (nib // 2), (nib % 2)
                    scl = d_sc[:, 2 * p + ph, :]
                    oview = d_wdo[:, 2 * p + pn, :].rearrange("a (b c) -> a b c", c=2)[:, :, ph]
                wlast[nib] = em.op("vector", lambda v, nib=nib, oview=oview, scl=scl:
                                   v.tensor_tensor(
                    oview, d_mag[:, nib, :].bitcast(BF16),
                    scl, ALU.mult), inc="SDV")
            self.dv_tgt[p] = wlast[3]
            self.prev_dv = wlast[3]
            wr = 0
            if kind in ("g", "u"):
                dst = sc["wg_dec"] if kind == "g" else sc["wu_dec"]
                for nib in range(4):
                    ph, pn = (nib % 2), (nib // 2)
                    hcx = 2 * m + ph
                    em.wait("scalar", "SDV", wlast[nib])
                    dap = bass.AP(dst.tensor, hcx * 128 * NMLP + 512 * q + 128 * pn,
                                  [[NMLP, 128], [256, 2], [1, 128]])
                    wr = em.dma("scalar", dap, d_out[:, 4 * p + nib, :], f"GDWR{p}")
            else:
                em.wait("scalar", "SDV", wlast[3])
                for pn in range(2):
                    r0 = 256 * m + 128 * pn
                    wr = em.dma("scalar", sc["wd_dec"][r0:r0 + 128, 512 * q:512 * q + 512],
                                d_wdo[:, 2 * p + pn, :], f"GDWR{p}")
            self.wr_tgt[p] = wr

    dec = Decode()

    xn_tm = carve(aren, 0, 8192, BF16, [2, 2048])
    xn_sb = carve(aren, 16384, 8192, BF16, [2, HC, 128])
    qs_sb = carve(aren, 24576, 8192, BF16, [2, D])

    qk_evac_dv = [0, 0, 0, 0]
    xn_pe = [0, 0]
    xn_wr = [0, 0]
    xntm_pe = [0, 0]
    qs_wr = [0, 0]
    xrow_act = 0
    xrow_dv = 0
    var_par = [0, 0]
    tcopy_act = [0, 0, 0, 0]

    stats_par = [stat_sb, bst_sb]

    for st in range(NSTRIP):
        tok0 = st * 128
        pp = st % 2
        stp = stats_par[pp]
        em.wait("sync", "SACT", xrow_act)
        em.wait("sync", "SDV", xrow_dv)
        ld_r = em.dma("sync", xrow_sb, x_tm[tok0:tok0 + 128, :], "LXR")

        em.wait("vector", "LXR", ld_r)
        rs_dv = em.op("vector", lambda v, stp=stp: v.reduce_sum(
            stp[:, 0:1], xrow_sb, axis=AX.X), inc="SDV")
        em.wait("scalar", "LXR", ld_r)
        em.wait("scalar", "SDV", var_par[pp])
        if st == 0:
            em.wait("scalar", "SDV", ones_dv)
        sq1_act = em.op("scalar", lambda s2, stp=stp: s2.activation(
            out=sqs_sb, in_=xrow_sb, func=AF.Square, accum_out=stp[:, 1:2]), inc="SACT")
        xrow_act = sq1_act
        em.wait("vector", "SDV", rs_dv)
        mean_dv = em.op("vector", lambda v, stp=stp: v.tensor_scalar(
            stp[:, 2:3], stp[:, 0:1], 1.0 / D, None, ALU.mult), inc="SDV")
        em.wait("vector", "SDV", mean_dv)
        m2_dv = em.op("vector", lambda v, stp=stp: v.tensor_scalar(
            stp[:, 3:4], stp[:, 2:3], stp[:, 2:3], None, ALU.mult), inc="SDV")
        em.wait("vector", "SDV", m2_dv)
        em.wait("vector", "SACT", sq1_act)
        var_dv = em.op("vector", lambda v, stp=stp: v.scalar_tensor_tensor(
            stp[:, 4:5], stp[:, 1:2], 1.0 / D, stp[:, 3:4],
            ALU.mult, ALU.subtract), inc="SDV")
        var_par[pp] = var_dv
        em.wait("scalar", "SDV", var_dv)
        sd_act = em.op("scalar", lambda s2, stp=stp: s2.activation(
            out=stp[:, 5:6], in_=stp[:, 4:5], func=AF.Sqrt,
            bias=eps_sb[:, 0:1]), inc="SACT")
        em.wait("vector", "SACT", sd_act)
        rc_dv = em.op("vector", lambda v, stp=stp: v.reciprocal(
            stp[:, 3:4], stp[:, 5:6]), inc="SDV")

        em.wait("vector", "SDV", rc_dv)
        em.wait("vector", "SPE", xntm_pe[pp])
        in4 = xrow_sb.rearrange("p (b i q) -> p b q i", b=8, i=128, q=2)
        out4 = xn_tm[:, pp].rearrange("p (b q i) -> p b q i", b=8, q=2, i=128)
        xntm_dv = em.op("vector", lambda v, in4=in4, out4=out4, stp=stp: v.tensor_scalar(
            out4, in4, stp[:, 2:3], stp[:, 3:4], ALU.subtract, ALU.mult), inc="SDV")
        xrow_dv = xntm_dv

        dec.emit(10)
        em.wait("tensor", "SDV", xntm_dv)
        if st == 0:
            em.wait("tensor", "GCAST", 32)
        cp_act = 0
        for hc in range(HC):
            bk = hc % 4
            tp_b = pb[bk].ap().bitcast(BF16)[:, 0:128]
            em.wait("tensor", "SACT", tcopy_act[bk])
            tpe = em.op("tensor", lambda tn, hc=hc, pp=pp, tp_b=tp_b: tn.transpose(
                tp_b, xn_tm[:, pp, 128 * hc:128 * hc + 128], idb_sb), inc="SPE")
            if hc == HC - 1:
                xntm_pe[pp] = tpe
            em.wait("scalar", "SPE", tpe)
            if hc == 0:
                em.wait("scalar", "SPE", xn_pe[pp])
                em.wait("scalar", f"GXW{pp}", xn_wr[pp])
            cp_act = em.op("scalar", lambda s2, hc=hc, pp=pp, tp_b=tp_b: s2.copy(
                xn_sb[:, pp, hc, :], tp_b), inc="SACT")
            tcopy_act[bk] = cp_act
        xn_ready = cp_act

        if st < 32:
            em.wait("gpsimd", "SACT", xn_ready)
            dstap = bass.AP(sc["xn_sc"].tensor, tok0, [[TLOC, 128], [128 * TLOC, HC], [1, 128]])
            xn_wr[pp] = em.dma("gpsimd", dstap, xn_sb[:, pp], f"GXW{pp}")

        em.wait("tensor", "SACT", xn_ready)
        if st == 0:
            em.wait("tensor", "GCAST", wqk_ld)
        for kq in range(2):
            wsb = wq_sb if kq == 0 else wk_sb
            mm = None
            for hc in range(HC):
                for e5 in range(4):
                    if hc == 0:
                        em.wait("tensor", "SDV", qk_evac_dv[e5])
                    last = (hc == HC - 1 and e5 == 3)
                    mm = em.op("tensor", lambda tn, wsb=wsb, hc=hc, e5=e5, pp=pp: tn.matmul(
                        pb[4 + e5][:, :], xn_sb[:, pp, hc, :],
                        wsb[:, hc, 512 * e5:512 * e5 + 512],
                        start=(hc == 0), stop=(hc == HC - 1)),
                        inc="SPE" if last else None)
            pe_qk = mm
            if kq == 1:
                xn_pe[pp] = pe_qk
            em.wait("vector", "SPE", pe_qk)
            em.wait("vector", f"GQW{kq}", qs_wr[kq])
            for e5 in range(4):
                qk_evac_dv[e5] = em.op("vector", lambda v, e5=e5, kq=kq: v.tensor_copy(
                    qs_sb[:, kq, 512 * e5:512 * e5 + 512], pb[4 + e5][:, :]), inc="SDV")
            em.wait("gpsimd", "SDV", qk_evac_dv[3])
            dst = sc["q_sc"] if kq == 0 else sc["k_sc"]
            qs_wr[kq] = em.dma("gpsimd", dst[tok0:tok0 + 128, :], qs_sb[:, kq], f"GQW{kq}")

    a_pe = em.now("SPE")
    dec_wr_all = (em.now("GDWR0"), em.now("GDWR1"))
    if last_phase == 'A':
        return _finish(nc, em, dbg, dbg_outs, sc)

    q_sb = carve(big0, 0, 65536, BF16, [64, 512])
    k_sb = carve(big1, 0, 32768, BF16, [2, 8, 1024])
    sco_sb = carve(big1, 32768, 16384, F32, [2, 2048])
    exp_sb = carve(big1, 49152, 16384, F32, [2, 2048])
    asb = carve(aren, 0, 8192, BF16, [2, 2048])

    blk_ev = [0, 0]
    k_pe = [0, 0]
    a_wr = 0
    bi_g = 0
    es_prev_pe = a_pe

    for es in range(4):
        em.wait("sync", "SPE", es_prev_pe)
        if es == 0:
            em.wait("sync", "GQW0", qs_wr[0])
            em.wait("sync", "GQW1", qs_wr[1])
        ldq = 0
        for g8 in range(8):
            qap = bass.AP(sc["q_sc"].tensor, (8 * g8) * 128 * D + 512 * es,
                          [[D, 128], [D * 128, 8], [1, 512]])
            ldq = em.dma("sync", q_sb[:, 8 * g8:8 * g8 + 8, :], qap, "LQS")
        em.wait("tensor", "LQS", ldq)
        for ep in range(2):
            for fh in range(2):
                seti = bi_g % 2
                bks = [pb[4 * seti + i] for i in range(4)]
                mm = None
                for sc8 in range(8):
                    kp = sc8 % 2
                    em.wait("sync", "SPE", k_pe[kp])
                    kap = bass.AP(sc["k_sc"].tensor, (1024 * sc8) * D + 1024 * fh,
                                  [[D, 128], [D * 128, 8], [1, 1024]])
                    ldk = em.dma("sync", k_sb[:, kp], kap, f"LKC{kp}")
                    em.wait("tensor", f"LKC{kp}", ldk)
                    if sc8 == 0:
                        em.wait("tensor", "SDV", blk_ev[seti])
                    for stt in range(8):
                        for e1 in range(2):
                            for f5 in range(2):
                                last = (stt == 7 and e1 == 1 and f5 == 1)
                                mm = em.op("tensor", lambda tn, stt=stt, e1=e1, f5=f5,
                                           sc8=sc8, ep=ep, kp=kp, bks=bks: tn.matmul(
                                    bks[2 * e1 + f5][:, :],
                                    q_sb[:, 8 * sc8 + stt,
                                         128 * (2 * ep + e1):128 * (2 * ep + e1) + 128],
                                    k_sb[:, kp, stt, 512 * f5:512 * f5 + 512],
                                    start=(sc8 == 0 and stt == 0),
                                    stop=(sc8 == 7 and stt == 7)),
                                    inc="SPE" if last else None)
                    k_pe[kp] = mm
                pe_blk = k_pe[1]
                em.wait("vector", "SPE", pe_blk)
                for e1 in range(2):
                    for f5 in range(2):
                        blk_ev[seti] = em.op("vector", lambda v, e1=e1, f5=f5, fh=fh,
                                             bks=bks: v.tensor_copy(
                            sco_sb[:, e1, 1024 * fh + 512 * f5:1024 * fh + 512 * f5 + 512],
                            bks[2 * e1 + f5][:, :]), inc="SDV")
                bi_g += 1
                dec.emit(8)
            em.wait("vector", "SDV", blk_ev[0])
            em.wait("vector", "SDV", blk_ev[1])
            em.wait("vector", "GAW", a_wr)
            for e1 in range(2):
                mx_dv = em.op("vector", lambda v, e1=e1: v.reduce_max(
                    bst_sb[:, 0:1], sco_sb[:, e1, :], axis=AX.X), inc="SDV")
                em.wait("vector", "SDV", mx_dv)
                sm1_dv = em.op("vector", lambda v: v.tensor_scalar(
                    bst_sb[:, 1:2], bst_sb[:, 0:1], -0.125, None, ALU.mult), inc="SDV")
                em.wait("scalar", "SDV", sm1_dv)
                ex_act = em.op("scalar", lambda s2, e1=e1: s2.activation(
                    out=exp_sb[:, e1, :], in_=sco_sb[:, e1, :], func=AF.Exp,
                    bias=bst_sb[:, 1:2], scale=0.125, accum_out=bst_sb[:, 2:3]), inc="SACT")
                em.wait("vector", "SACT", ex_act)
                rc2_dv = em.op("vector", lambda v: v.reciprocal(
                    bst_sb[:, 3:4], bst_sb[:, 2:3]), inc="SDV")
                em.wait("vector", "SDV", rc2_dv)
                a_dv = em.op("vector", lambda v, e1=e1: v.tensor_scalar(
                    asb[:, e1, :], exp_sb[:, e1, :], bst_sb[:, 3:4], None, ALU.mult), inc="SDV")
                em.wait("gpsimd", "SDV", a_dv)
                erow = 512 * es + 256 * ep + 128 * e1
                a_wr = em.dma("gpsimd", sc["a_sc"][erow:erow + 128, :], asb[:, e1, :], "GAW")
        es_prev_pe = em.now("SPE")

    b_pe = em.now("SPE")
    if last_phase == 'B':
        dec.emit(999)
        return _finish(nc, em, dbg, dbg_outs, sc)

    wv_sb = carve(big0, 0, 65536, BF16, [HC, D])
    em.wait("gpsimd", "SPE", b_pe)
    wv_ld = em.dma("gpsimd", wv_sb, wrows(wv_i), "GCAST")
    at_sb = carve(big1, 0, 65536, BF16, [HC, 2048])
    vtb_sb = carve(aren, 0, 16384, BF16, [HC, 512])
    xnc = carve(aren, 16384, 16384, BF16, [HC, 512])
    arow = ist_sb.rearrange("p a b -> p (a b)")
    c2st = ist_sb

    tb_dv = [0, 0]
    at_ev = 0
    for ec in range(HC):
        if ec == 0:
            em.wait("sync", "GAW", a_wr)
            em.wait("sync", "SPE", b_pe)
        else:
            em.wait("sync", "SDV", at_ev)
        lda = em.dma("sync", arow, sc["a_sc"][128 * ec:128 * ec + 128, :], "LAR")
        em.wait("tensor", "LAR", lda)
        for fc in range(HC):
            tp_b = pb[4 + fc % 2].ap().bitcast(BF16)[:, 0:128]
            em.wait("tensor", "SDV", tb_dv[fc % 2])
            tpe = em.op("tensor", lambda tn, fc=fc, tp_b=tp_b: tn.transpose(
                tp_b, arow[:, 128 * fc:128 * fc + 128], idb_sb), inc="SPE")
            em.wait("vector", "SPE", tpe)
            tb_dv[fc % 2] = em.op("vector", lambda v, fc=fc, ec=ec, tp_b=tp_b: v.tensor_copy(
                at_sb[:, fc, 128 * ec:128 * ec + 128], tp_b), inc="SDV")
        at_ev = tb_dv[(HC - 1) % 2]
    at_done_dv = em.now("SDV")

    vban_dv = [0, 0, 0, 0]
    c2_wr = [0, 0]
    ob_dv = [0, 0, 0, 0]
    xnc_pe = 0
    for sb8 in range(8):
        em.wait("sync", "SPE", xnc_pe if sb8 > 0 else b_pe)
        if sb8 == 0:
            em.wait("sync", "SDV", at_done_dv)
        xap = bass.AP(sc["xn_sc"].tensor, 512 * sb8, [[TLOC, 128], [128 * TLOC, HC], [1, 512]])
        ldxn = em.dma("sync", xnc, xap, "LXC")
        em.wait("tensor", "LXC", ldxn)
        if sb8 == 0:
            em.wait("tensor", "GCAST", wv_ld)
            em.wait("tensor", "SDV", at_done_dv)
        for fc in range(HC):
            bk = fc % 4
            mm = None
            for hc in range(HC):
                if hc == 0:
                    em.wait("tensor", "SDV", vban_dv[bk])
                mm = em.op("tensor", lambda tn, fc=fc, hc=hc, bk=bk: tn.matmul(
                    pb[bk][:, :], wv_sb[:, hc, 128 * fc:128 * fc + 128],
                    xnc[:, hc, :], start=(hc == 0), stop=(hc == HC - 1)),
                    inc="SPE" if hc == HC - 1 else None)
            vpe = mm
            if fc == HC - 1:
                xnc_pe = vpe
            em.wait("vector", "SPE", vpe)
            vban_dv[bk] = em.op("vector", lambda v, fc=fc, bk=bk: v.tensor_copy(
                vtb_sb[:, fc, :], pb[bk][:, :]), inc="SDV")
        vt_dv = em.now("SDV")
        em.wait("tensor", "SDV", vt_dv)
        for ec in range(HC):
            bk = 4 + ec % 4
            mm = None
            for fc in range(HC):
                if fc == 0:
                    em.wait("tensor", "SDV", ob_dv[ec % 4])
                mm = em.op("tensor", lambda tn, ec=ec, fc=fc, bk=bk: tn.matmul(
                    pb[bk][:, :], at_sb[:, fc, 128 * ec:128 * ec + 128],
                    vtb_sb[:, fc, :], start=(fc == 0), stop=(fc == HC - 1)),
                    inc="SPE" if fc == HC - 1 else None)
            mpe = mm
            em.wait("vector", "SPE", mpe)
            em.wait("vector", f"GC2W{ec % 2}", c2_wr[ec % 2])
            ob_dv[ec % 4] = em.op("vector", lambda v, ec=ec, bk=bk: v.tensor_copy(
                c2st[:, ec % 2, :], pb[bk][:, :]), inc="SDV")
            em.wait("gpsimd", "SDV", ob_dv[ec % 4])
            off = 512 * sb8
            c2_wr[ec % 2] = em.dma("gpsimd", sc["attnT_sc"][ec, :, off:off + 512],
                                   c2st[:, ec % 2, :], f"GC2W{ec % 2}")

    c2_pe = em.now("SPE")
    c2_wr_all = (em.now("GC2W0"), em.now("GC2W1"))
    if last_phase == 'C':
        return _finish(nc, em, dbg, dbg_outs, sc)

    wo_sb = carve(big0, 0, 65536, BF16, [HC, D])
    em.wait("gpsimd", "SPE", c2_pe)
    wo_ld = em.dma("gpsimd", wo_sb, wrows(wo_i), "GCAST")
    atn_sb = carve(big1, 0, 32768, BF16, [2, HC, 512])
    af_st = carve(aren, 0, 8192, F32, [4, 512])

    atn_pe = [0, 0]
    c3ob_dv = [0, 0, 0, 0, 0, 0, 0, 0]
    af_wr = [0, 0]
    for s5 in range(8):
        ap_ = s5 % 2
        em.wait("sync", "SPE", atn_pe[ap_])
        if s5 == 0:
            em.wait("sync", "GC2W0", c2_wr_all[0])
            em.wait("sync", "GC2W1", c2_wr_all[1])
            em.wait("sync", "SPE", c2_pe)
        aap = bass.AP(sc["attnT_sc"].tensor, 512 * s5,
                      [[TLOC, 128], [128 * TLOC, HC], [1, 512]])
        lda = em.dma("sync", atn_sb[:, ap_], aap, f"LAT{ap_}")
        em.wait("tensor", f"LAT{ap_}", lda)
        if s5 == 0:
            em.wait("tensor", "GCAST", wo_ld)
        bset = 4 * (s5 % 2)
        for st1 in range(4):
            mm = None
            for ec in range(HC):
                for g5 in range(4):
                    if ec == 0:
                        em.wait("tensor", "SDV", c3ob_dv[bset + g5])
                    last = (ec == HC - 1 and g5 == 3)
                    mm = em.op("tensor", lambda tn, ap_=ap_, st1=st1, ec=ec, g5=g5,
                               bset=bset: tn.matmul(
                        pb[bset + g5][:, :],
                        atn_sb[:, ap_, ec, 128 * st1:128 * st1 + 128],
                        wo_sb[:, ec, 512 * g5:512 * g5 + 512],
                        start=(ec == 0), stop=(ec == HC - 1)),
                        inc="SPE" if last else None)
            mpe = mm
            atn_pe[ap_] = mpe
            em.wait("vector", "SPE", mpe)
            em.wait("vector", f"GAFW{s5 % 2}", af_wr[s5 % 2])
            for g5 in range(4):
                c3ob_dv[bset + g5] = em.op("vector", lambda v, g5=g5, bset=bset: v.tensor_copy(
                    af_st[:, g5, :], pb[bset + g5][:, :]), inc="SDV")
            em.wait("gpsimd", "SDV", c3ob_dv[bset + 3])
            tok0 = 512 * s5 + 128 * st1
            af_wr[s5 % 2] = em.dma("gpsimd", sc["afin_sc"][tok0:tok0 + 128, :],
                                   af_st.rearrange("p a b -> p (a b)"), f"GAFW{s5 % 2}")

    dec.emit(999)
    c3_pe = em.now("SPE")
    c3_wr_all = (em.now("GAFW0"), em.now("GAFW1"))
    if last_phase == 'C3':
        return _finish(nc, em, dbg, dbg_outs, sc)

    xnh_sb = carve(big0, 0, 65536, BF16, [HC, 2048])
    wgu_sb = carve(big1, 0, 32768, BF16, [2, 2, HC, 256])

    gel_dv = [0, 0, 0, 0]
    sig_war = [0, 0, 0, 0]
    ist_wr = [0, 0, 0, 0]
    wgu_pe = [0, 0]
    d_first = True
    for dh in range(2):
        em.wait("sync", "SPE", c3_pe if dh == 0 else em.now("SPE"))
        ldh = 0
        for hc in range(HC):
            xap = bass.AP(sc["xn_sc"].tensor, hc * 128 * TLOC + 2048 * dh,
                          [[TLOC, 128], [1, 2048]])
            ldh = em.dma("sync", xnh_sb[:, hc, :], xap, "LXH")
        for ntp in range(64):
            wp = ntp % 2
            em.wait("sync", "SPE", wgu_pe[wp])
            if d_first:
                em.wait("sync", "GDWR0", dec_wr_all[0])
                em.wait("sync", "GDWR1", dec_wr_all[1])
                d_first = False
            gap = bass.AP(sc["wg_dec"].tensor, 256 * ntp, [[NMLP, 128], [128 * NMLP, HC], [1, 256]])
            uap = bass.AP(sc["wu_dec"].tensor, 256 * ntp, [[NMLP, 128], [128 * NMLP, HC], [1, 256]])
            em.dma("sync", wgu_sb[:, 0, wp], gap, f"LWG{wp}")
            ldw = em.dma("sync", wgu_sb[:, 1, wp], uap, f"LWG{wp}")
            for nt1 in range(2):
                nt = 2 * ntp + nt1
                em.wait("tensor", f"LWG{wp}", ldw)
                if ntp == 0 and nt1 == 0:
                    em.wait("tensor", "LXH", ldh)
                gpe = upe = None
                for gu in range(2):
                    mm = None
                    for hc in range(HC):
                        for t5 in range(4):
                            if hc == 0:
                                em.wait("tensor", "SDV", gel_dv[t5])
                            last = (hc == HC - 1 and t5 == 3)
                            mm = em.op("tensor", lambda tn, gu=gu, hc=hc, t5=t5, wp=wp,
                                       nt1=nt1: tn.matmul(
                                pb[4 * gu + t5][:, :],
                                wgu_sb[:, gu, wp, hc, 128 * nt1:128 * nt1 + 128],
                                xnh_sb[:, hc, 512 * t5:512 * t5 + 512],
                                start=(hc == 0), stop=(hc == HC - 1)),
                                inc="SPE" if last else None)
                    if gu == 0:
                        gpe = mm
                    else:
                        upe = mm
                wgu_pe[wp] = upe
                for t5 in range(4):
                    gb, ub = pb[t5], pb[4 + t5]
                    gslot = gel_sb[:, t5, :]
                    qslot = sqs_sb[:, 512 * t5:512 * t5 + 512]
                    em.wait("scalar", "SPE", gpe)
                    em.wait("scalar", "SDV", gel_dv[t5])
                    s2m = em.op("scalar", lambda s2_, gb=gb, gslot=gslot: s2_.activation(
                        out=gslot, in_=gb[:, :], func=AF.Square), inc="SACT")
                    em.wait("vector", "SACT", s2m)
                    em.wait("vector", f"GIW{t5}", ist_wr[t5])
                    uum = em.op("vector", lambda v, gslot=gslot: v.tensor_scalar(
                        gslot, gslot, GELU_B, 1.0, ALU.mult, ALU.add), inc="SDV")
                    em.wait("vector", "SDV", uum)
                    w1m = em.op("vector", lambda v, gb=gb, gslot=gslot: v.tensor_tensor(
                        gslot, gslot, gb[:, :], ALU.mult), inc="SDV")
                    em.wait("scalar", "SDV", w1m)
                    em.wait("scalar", "SDV", sig_war[t5])
                    sgm = em.op("scalar", lambda s2, gslot=gslot, qslot=qslot: s2.activation(
                        out=qslot, in_=gslot, func=AF.Sigmoid, scale=GELU_A), inc="SACT")
                    em.wait("vector", "SACT", sgm)
                    w2m = em.op("vector", lambda v, gb=gb, gslot=gslot, qslot=qslot:
                                v.tensor_tensor(gslot, qslot, gb[:, :], ALU.mult), inc="SDV")
                    sig_war[t5] = w2m
                    em.wait("vector", "SDV", w2m)
                    em.wait("vector", "SPE", upe)
                    gel_dv[t5] = em.op("vector", lambda v, ub=ub, t5=t5, gslot=gslot:
                                       v.tensor_tensor(
                        ist_sb[:, t5, :], gslot, ub[:, :], ALU.mult), inc="SDV")
                    em.wait("gpsimd", "SDV", gel_dv[t5])
                    off = 2048 * dh + 512 * t5
                    ist_wr[t5] = em.dma("gpsimd", sc["inter_sc"][nt, :, off:off + 512],
                                        ist_sb[:, t5, :], f"GIW{t5}")

    d_pe = em.now("SPE")
    d_wr_all = tuple(em.now(f"GIW{i}") for i in range(4))
    if last_phase == 'D':
        return _finish(nc, em, dbg, dbg_outs, sc)

    wdt_lo = carve(big0, 0, 65536, BF16, [64, 512])
    wdt_hi = carve(big1, 0, 65536, BF16, [64, 512])

    def wdt(nt):
        return wdt_lo[:, nt, :] if nt < 64 else wdt_hi[:, nt - 64, :]

    ich_sb = carve(aren, 0, 32768, BF16, [2, HC, 512])
    xs_sb = xrow_sb.rearrange("p (a b) -> p a b", a=4)
    as_sb = sqs_sb.rearrange("p (a b) -> p a b", a=4)

    e_first = True
    ich_pe = [0, 0]
    eob_dv = [0, 0, 0, 0, 0, 0, 0, 0]
    o_wr = [0, 0]
    xa_dv = 0
    wdt_pe = 0
    for hp in range(4):
        em.wait("sync", "SPE", wdt_pe)
        if e_first:
            em.wait("sync", "SPE", d_pe)
            for i4 in range(4):
                em.wait("sync", f"GIW{i4}", d_wr_all[i4])
            e_first = False
        ldwd = 0
        for i in range(16):
            wap = bass.AP(sc["wd_dec"].tensor, (1024 * i) * D + 512 * hp,
                          [[D, 128], [D * 128, 8], [1, 512]])
            ldwd = em.dma("sync",
                          (wdt_lo[:, 8 * i:8 * i + 8, :] if i < 8
                           else wdt_hi[:, 8 * (i - 8):8 * (i - 8) + 8, :]),
                          wap, "LWD")
        for t5 in range(8):
            bset = 4 * (t5 % 2)
            em.wait("sync", "SDV", xa_dv)
            if hp == 0 and t5 == 0:
                em.wait("sync", "GAFW0", c3_wr_all[0])
                em.wait("sync", "GAFW1", c3_wr_all[1])
            xap = bass.AP(x_tm.tensor, (512 * t5) * D + 512 * hp,
                          [[D, 128], [D * 128, 4], [1, 512]])
            aap = bass.AP(sc["afin_sc"].tensor, (512 * t5) * D + 512 * hp,
                          [[D, 128], [D * 128, 4], [1, 512]])
            em.dma("sync", xs_sb, xap, "LXA")
            ldxa = em.dma("sync", as_sb, aap, "LXA")
            mm = None
            for e8 in range(8):
                ip = e8 % 2
                em.wait("sync", "SPE", ich_pe[ip])
                iap = bass.AP(sc["inter_sc"].tensor, (16 * e8) * 128 * TLOC + 512 * t5,
                              [[TLOC, 128], [128 * TLOC, HC], [1, 512]])
                ldi = em.dma("sync", ich_sb[:, ip], iap, f"LIC{ip}")
                em.wait("tensor", f"LIC{ip}", ldi)
                if e8 == 0:
                    em.wait("tensor", "LWD", ldwd)
                    for t1 in range(4):
                        em.wait("tensor", "SDV", eob_dv[bset + t1])
                for ntl in range(HC):
                    nt = 16 * e8 + ntl
                    for t1 in range(4):
                        last = (ntl == HC - 1 and t1 == 3)
                        mm = em.op("tensor", lambda tn, ip=ip, ntl=ntl, nt=nt, t1=t1,
                                   bset=bset: tn.matmul(
                            pb[bset + t1][:, :],
                            ich_sb[:, ip, ntl, 128 * t1:128 * t1 + 128],
                            wdt(nt), start=(nt == 0), stop=(nt == 127)),
                            inc="SPE" if last else None)
                ich_pe[ip] = mm
            mpe = ich_pe[1]
            if t5 == 7:
                wdt_pe = mpe
            em.wait("vector", "SPE", mpe)
            em.wait("vector", "LXA", ldxa)
            for t1 in range(4):
                osl = 2 + t1 % 2
                em.wait("vector", f"GOW{t1 % 2}", o_wr[t1 % 2])
                r1 = em.op("vector", lambda v, t1=t1, bset=bset: v.tensor_tensor(
                    gel_sb[:, 0, :], as_sb[:, t1, :], pb[bset + t1][:, :], ALU.add), inc="SDV")
                em.wait("vector", "SDV", r1)
                eob_dv[bset + t1] = em.op("vector", lambda v, t1=t1, osl=osl: v.tensor_tensor(
                    gel_sb[:, osl, :], gel_sb[:, 0, :], xs_sb[:, t1, :], ALU.add), inc="SDV")
                em.wait("gpsimd", "SDV", eob_dv[bset + t1])
                tok0 = 512 * t5 + 128 * t1
                o_wr[t1 % 2] = em.dma("gpsimd",
                                      out_loc[tok0:tok0 + 128, 512 * hp:512 * hp + 512],
                                      gel_sb[:, osl, :], f"GOW{t1 % 2}")
            xa_dv = eob_dv[bset + 3]

    return _finish(nc, em, dbg, dbg_outs, sc)


def _finish(nc, em, dbg, dbg_outs, sc):
    for name in dbg:
        em.dma("gpsimd", dbg_outs[name], sc[name], "GDBG")
    for s_ in em.dma_sems:
        eng = "gpsimd" if s_.startswith("G") else "sync"
        em.wait(eng, s_, em.now(s_))
    with nc.Block() as block:
        @block.sync
        def _(e):
            for f in em.streams["sync"]:
                f(e)

        @block.gpsimd
        def _(e):
            for f in em.streams["gpsimd"]:
                f(e)

        @block.scalar
        def _(e):
            for f in em.streams["scalar"]:
                f(e)

        @block.vector
        def _(e):
            for f in em.streams["vector"]:
                f(e)

        @block.tensor
        def _(e):
            for f in em.streams["tensor"]:
                f(e)
    counts = {k: len(v) for k, v in em.streams.items()}
    return nc, counts


def host_inputs(inp):
    import ml_dtypes
    perm = _perm_h()
    x = np.asarray(inp["L0_attn_x"], dtype=np.float32)
    wq = np.asarray(inp["L0_attn_Wq"], dtype=np.float32)
    wk = np.asarray(inp["L0_attn_Wk"], dtype=np.float32)
    wv = np.asarray(inp["L0_attn_Wv"], dtype=np.float32)
    wo = np.asarray(inp["L0_attn_Wo"], dtype=np.float32)

    def pack_t(p_int32):
        b = np.ascontiguousarray(p_int32).view(np.uint8).reshape(
            p_int32.shape[0], p_int32.shape[1], 4)[:, :, 0]
        bt = np.ascontiguousarray(b.T)
        return bt.view("<i2")

    pg = pack_t(np.asarray(inp["L0_ffn_gate_W_packed"]))
    pu = pack_t(np.asarray(inp["L0_ffn_up_W_packed"]))
    pd = pack_t(np.asarray(inp["L0_ffn_down_W_packed"]))
    sgt = np.ascontiguousarray(np.asarray(inp["L0_ffn_gate_scales"], dtype=np.float32).T)
    sut = np.ascontiguousarray(np.asarray(inp["L0_ffn_up_scales"], dtype=np.float32).T)
    sdt = np.ascontiguousarray(np.asarray(inp["L0_ffn_down_scales"], dtype=np.float32).T)
    sg0 = np.ascontiguousarray(sgt[:, 0::2])
    sg1 = np.ascontiguousarray(sgt[:, 1::2])
    su0 = np.ascontiguousarray(sut[:, 0::2])
    su1 = np.ascontiguousarray(sut[:, 1::2])
    sd0 = np.ascontiguousarray(sdt[:, 0::2])
    sd1 = np.ascontiguousarray(sdt[:, 1::2])
    shared = {
        "wq": np.ascontiguousarray(wq[perm]),
        "wk": np.ascontiguousarray(wk[perm]),
        "wv": np.ascontiguousarray(wv[perm]),
        "wo": np.ascontiguousarray(wo),
        "idb": np.eye(128, dtype=np.float32).astype(ml_dtypes.bfloat16),
    }
    maps = []
    for c in range(8):
        b, h = c // 2, c % 2
        lf = np.concatenate([np.arange(h * 4096, (h + 1) * 4096),
                             np.arange((1 - h) * 4096, (2 - h) * 4096)])
        xb = x[b][lf]
        m = dict(shared)
        m["x_tm"] = np.ascontiguousarray(xb)
        m["pg16"] = pg
        m["pu16"] = pu
        m["pd16"] = pd
        m["sg0"] = sg0
        m["sg1"] = sg1
        m["su0"] = su0
        m["su1"] = su1
        m["sd0"] = sd0
        m["sd1"] = sd1
        maps.append(m)
    return maps


def kernel(**inputs):
    nc, _ = build_program(dbg=DBG)
    maps = host_inputs(inputs)
    res = run_bass_kernel_spmd(nc, maps, core_ids=list(range(8)))
    out = np.empty((B_, S, D), dtype=np.float32)
    for c in range(8):
        b, h = c // 2, c % 2
        out[b, h * 4096:(h + 1) * 4096, :] = res.results[c]["out_loc"]
    return out


if __name__ == "__main__":
    nc, counts = build_program()
    print("instruction closures per engine:", counts)


# version 15
# speedup vs baseline: 1.0924x; 1.0493x over previous
import numpy as np
import concourse.bass as bass
import concourse.mybir as mybir
from concourse.bass_utils import run_bass_kernel_spmd

F32 = mybir.dt.float32
BF16 = mybir.dt.bfloat16
I16 = mybir.dt.int16
U8 = mybir.dt.uint8
AF = mybir.ActivationFunctionType
ALU = mybir.AluOpType
AX = mybir.AxisListType

B_, S, D = 4, 8192, 2048
NMLP = 16384
TLOC = 4096
HC = 16
NSTRIP = 64
GELU_A = 1.5957691216056908
GELU_B = 0.044715

DBG = []


def _perm_h():
    p = []
    for m in range(8):
        p += [256 * m + 2 * i for i in range(128)]
        p += [256 * m + 2 * i + 1 for i in range(128)]
    return np.array(p, dtype=np.int64)


class Emit:
    def __init__(self, nc):
        self.nc = nc
        self.streams = {k: [] for k in ("sync", "gpsimd", "scalar", "vector", "tensor")}
        self.sems = {}
        self.cnt = {}

    def sem(self, name):
        h = self.nc.semaphore(name).__enter__()
        self.sems[name] = h
        self.cnt[name] = 0

    def add(self, eng, fn):
        self.streams[eng].append(fn)

    def dma(self, eng, out_ap, in_ap, sem_name):
        h = self.sems[sem_name]
        self.cnt[sem_name] += 16
        tgt = self.cnt[sem_name]

        def _f(e, out_ap=out_ap, in_ap=in_ap, h=h):
            e.dma_start(out=out_ap, in_=in_ap).then_inc(h, 16)
        self.add(eng, _f)
        return tgt

    def op(self, eng, builder, inc=None):
        tgt, h = None, None
        if inc is not None:
            self.cnt[inc] += 1
            tgt = self.cnt[inc]
            h = self.sems[inc]

        def _f(e, builder=builder, h=h):
            ins = builder(e)
            if h is not None:
                ins.then_inc(h, 1)
        self.add(eng, _f)
        return tgt

    def wait(self, eng, sem_name, val):
        if not val:
            return
        h = self.sems[sem_name]

        def _f(e, h=h, val=val):
            e.wait_ge(h, val)
        self.add(eng, _f)

    def now(self, sem_name):
        return self.cnt[sem_name]


def build_program(dbg=(), last_phase='E'):
    nc = bass.Bass("TRN2", target_bir_lowering=False)
    em = Emit(nc)
    DMA_SEMS = ["LXT0", "LXT1", "LXR", "LQS", "LKC0", "LKC1", "LAR", "LXC",
                "LAT0", "LAT1", "LXH", "LWG0", "LWG1", "LWD", "LIC0", "LIC1", "LXA",
                "GCAST", "GDEC0", "GDEC1", "GDWR0", "GDWR1", "GXW0", "GXW1",
                "GQW0", "GQW1", "GAW", "GC2W0", "GC2W1", "GAFW0", "GAFW1",
                "GIW0", "GIW1", "GIW2", "GIW3", "GOW0", "GOW1", "GDBG", "GAGR"]
    for s_ in ["SPE", "SDV", "SACT"] + DMA_SEMS:
        em.sem(s_)
    em.dma_sems = DMA_SEMS

    x_tm = nc.dram_tensor("x_tm", [S, D], F32, kind="ExternalInput").ap()
    wq_i = nc.dram_tensor("wq", [D, D], F32, kind="ExternalInput").ap()
    wk_i = nc.dram_tensor("wk", [D, D], F32, kind="ExternalInput").ap()
    wv_i = nc.dram_tensor("wv", [D, D], F32, kind="ExternalInput").ap()
    wo_i = nc.dram_tensor("wo", [D, D], F32, kind="ExternalInput").ap()
    pg16 = nc.dram_tensor("pg16", [1024, 8192], I16, kind="ExternalInput").ap()
    pu16 = nc.dram_tensor("pu16", [1024, 8192], I16, kind="ExternalInput").ap()
    pd16 = nc.dram_tensor("pd16", [8192, 1024], I16, kind="ExternalInput").ap()
    sg_s = [nc.dram_tensor(f"sg{p}", [64, 8192], F32, kind="ExternalInput").ap() for p in range(2)]
    su_s = [nc.dram_tensor(f"su{p}", [64, 8192], F32, kind="ExternalInput").ap() for p in range(2)]
    sd_s = [nc.dram_tensor(f"sd{p}", [512, 1024], F32, kind="ExternalInput").ap() for p in range(2)]
    idb_i = nc.dram_tensor("idb", [128, 128], BF16, kind="ExternalInput").ap()
    out_loc = nc.dram_tensor("out_loc", [TLOC, D], F32, kind="ExternalOutput").ap()

    sc = {
        "xn_sc": nc.dram_tensor("xn_sc", [HC, 128, TLOC], BF16).ap(),
        "q_sc": nc.dram_tensor("q_sc", [S, D], BF16).ap(),
        "k_sc": nc.dram_tensor("k_sc", [S, D], BF16).ap(),
        "a_sc": nc.dram_tensor("a_sc", [D, D], BF16).ap(),
        "attnT_sc": nc.dram_tensor("attnT_sc", [HC, 128, TLOC], BF16).ap(),
        "afin_sc": nc.dram_tensor("afin_sc", [TLOC, D], F32).ap(),
        "wg_dec": nc.dram_tensor("wg_dec", [HC, 128, NMLP], BF16).ap(),
        "wu_dec": nc.dram_tensor("wu_dec", [HC, 128, NMLP], BF16).ap(),
        "wd_dec": nc.dram_tensor("wd_dec", [NMLP, D], BF16).ap(),
        "inter_sc": nc.dram_tensor("inter_sc", [128, 128, TLOC], BF16).ap(),
    }
    dbg_outs = {}
    for name in dbg:
        t = sc[name].tensor
        dbg_outs[name] = nc.dram_tensor("dbg_" + name, list(t.shape), t.dtype,
                                        kind="ExternalOutput").ap()

    big0 = nc.alloc_sbuf_tensor("big0", [128, 65536], U8).ap()
    big1 = nc.alloc_sbuf_tensor("big1", [128, 65536], U8).ap()
    aren = nc.alloc_sbuf_tensor("aren", [128, 32768], U8).ap()

    def carve(base, off, nbytes, dtype, shape):
        ap = base[:, off:off + nbytes].bitcast(dtype)
        if len(shape) == 1:
            return ap
        spec = "p (" + " ".join(f"d{i}" for i in range(len(shape))) + ") -> p " + \
               " ".join(f"d{i}" for i in range(len(shape)))
        return ap.rearrange(spec, **{f"d{i}": shape[i] for i in range(len(shape))})

    xrow_sb = nc.alloc_sbuf_tensor("xrow", [128, D], F32).ap()
    sqs_sb = nc.alloc_sbuf_tensor("sqs", [128, D], F32).ap()
    stat_sb = nc.alloc_sbuf_tensor("stat", [128, 8], F32).ap()
    bst_sb = nc.alloc_sbuf_tensor("bst", [128, 8], F32).ap()
    ones_sb = nc.alloc_sbuf_tensor("ones1", [1, 128], F32).ap()
    eps_sb = nc.alloc_sbuf_tensor("eps1", [128, 1], F32).ap()
    idb_sb = nc.alloc_sbuf_tensor("idb_sb", [128, 128], BF16).ap()
    d_in = nc.alloc_sbuf_tensor("d_in", [128, 2, 256], I16).ap()
    d_t = nc.alloc_sbuf_tensor("d_t", [128, 4, 256], I16).ap()
    d_mag = nc.alloc_sbuf_tensor("d_mag", [128, 4, 256], I16).ap()
    d_sb2 = nc.alloc_sbuf_tensor("d_sb2", [128, 4, 256], I16).ap()
    d_t3 = nc.alloc_sbuf_tensor("d_t3", [128, 4, 256], BF16).ap()
    d_out = nc.alloc_sbuf_tensor("d_out", [128, 8, 256], BF16).ap()
    d_wdo = nc.alloc_sbuf_tensor("d_wdo", [128, 4, 512], BF16).ap()
    ist_sb = nc.alloc_sbuf_tensor("ist", [128, 4, 512], BF16).ap()
    gel_sb = nc.alloc_sbuf_tensor("gel", [128, 4, 512], F32).ap()

    d_sc = gel_sb.rearrange("p a b -> p (a b)")[:, 0:1024].rearrange(
        "p (a b) -> p a b", a=4)
    pb = [nc.alloc_psum_tensor(f"pb{i}", [128, 512], F32) for i in range(8)]

    em.op("vector", lambda v: v.memset(ones_sb, 1.0))
    ones_dv = em.op("vector", lambda v: v.memset(eps_sb, 1e-5), inc="SDV")
    em.dma("gpsimd", idb_sb, idb_i, "GCAST")
    wq_sb = carve(big0, 0, 65536, BF16, [HC, D])
    wk_sb = carve(big1, 0, 65536, BF16, [HC, D])

    def wrows(w_ap):
        return bass.AP(w_ap.tensor, 0, [[D, 128], [128 * D, HC], [1, D]])
    em.dma("gpsimd", wq_sb, wrows(wq_i), "GCAST")
    wqk_ld = em.dma("gpsimd", wk_sb, wrows(wk_i), "GCAST")

    class Decode:

        def __init__(self):
            self.jobs = []
            for q in range(32):
                for m in range(8):
                    self.jobs.append(("g", m, q))
            for q in range(32):
                for m in range(8):
                    self.jobs.append(("u", m, q))
            for q in range(4):
                for m in range(64):
                    self.jobs.append(("d", m, q))
            self.k = 0
            self.wr_tgt = [0, 0]
            self.dv_tgt = [0, 0]
            self.prev_dv = 0
            self.relu_war = 0

        def done(self):
            return self.k >= len(self.jobs)

        def emit(self, n):
            for _ in range(n):
                if self.done():
                    return
                self._one(self.k)
                self.k += 1

        def _one(self, k):
            kind, m, q = self.jobs[k]
            p = k % 2
            din = d_in[:, p, :]
            if self.dv_tgt[p]:
                em.wait("gpsimd", "SDV", self.dv_tgt[p])
            dsem = f"GDEC{p}"
            if kind in ("g", "u"):
                src = pg16 if kind == "g" else pu16
                scs = sg_s if kind == "g" else su_s
                em.dma("gpsimd", din, src[128 * m:128 * m + 128, 256 * q:256 * q + 256], dsem)
                ld2 = 0
                for pn in range(2):
                    sap = bass.AP(scs[pn].tensor, 8192 * 8 * m + 256 * q,
                                  [[8192, 8], [0, 16], [1, 256]])
                    ld2 = em.dma("gpsimd", d_sc[:, 2 * p + pn, :], sap, dsem)
            else:
                em.dma("gpsimd", din, pd16[128 * m:128 * m + 128, 256 * q:256 * q + 256], dsem)
                ld2 = 0
                for ph in range(2):
                    sap = bass.AP(sd_s[ph].tensor, 1024 * 8 * m + 256 * q,
                                  [[1024, 8], [0, 16], [1, 256]])
                    ld2 = em.dma("gpsimd", d_sc[:, 2 * p + ph, :], sap, dsem)
            em.wait("vector", dsem, ld2)
            if self.wr_tgt[p]:
                em.wait("vector", f"GDWR{p}", self.wr_tgt[p])

            e1 = 0
            for nib in range(4):
                e1 = em.op("vector", lambda v, sh=4 * nib, nib=nib, din=din: v.tensor_scalar(
                    d_t[:, nib, :], din, sh, 7, ALU.logical_shift_right, ALU.bitwise_and),
                    inc="SDV")
            em.wait("vector", "SDV", e1)
            e2 = 0
            for nib in range(4):
                e2 = em.op("vector", lambda v, nib=nib: v.tensor_scalar(
                    d_mag[:, nib, :], d_t[:, nib, :], 64, 16128, ALU.mult, ALU.add), inc="SDV")
            e3 = 0
            for nib in range(4):
                msk = 8 << (4 * nib)
                if msk >= 32768:
                    msk -= 65536
                e3 = em.op("vector", lambda v, msk=msk, shl=12 - 4 * nib, nib=nib, din=din:
                           v.tensor_scalar(d_sb2[:, nib, :], din, msk, shl,
                                           ALU.bitwise_and, ALU.logical_shift_left), inc="SDV")
            em.wait("scalar", "SDV", e2)
            em.wait("scalar", "SDV", self.relu_war)
            a1 = 0
            for nib in range(4):
                a1 = em.op("scalar", lambda s2, nib=nib: s2.activation(
                    out=d_t3[:, nib, :], in_=d_mag[:, nib, :].bitcast(BF16),
                    func=AF.Relu, scale=-1.0, bias=1.0), inc="SACT")
            em.wait("vector", "SACT", a1)
            e4 = 0
            for nib in range(4):
                e4 = em.op("vector", lambda v, nib=nib: v.tensor_tensor(
                    d_t[:, nib, :].bitcast(BF16),
                    d_mag[:, nib, :].bitcast(BF16),
                    d_t3[:, nib, :], ALU.subtract), inc="SDV")
            self.relu_war = e4
            em.wait("vector", "SDV", e4)
            em.wait("vector", "SDV", e3)
            e5 = 0
            for nib in range(4):
                e5 = em.op("vector", lambda v, nib=nib: v.tensor_tensor(
                    d_mag[:, nib, :], d_t[:, nib, :], d_sb2[:, nib, :], ALU.bitwise_or), inc="SDV")
            em.wait("vector", "SDV", e5)
            wlast = [0, 0, 0, 0]
            for nib in range(4):
                if kind in ("g", "u"):
                    ph, pn = (nib % 2), (nib // 2)
                    scl = d_sc[:, 2 * p + pn, :]
                    oview = d_out[:, 4 * p + nib, :]
                else:
                    ph, pn = (nib // 2), (nib % 2)
                    scl = d_sc[:, 2 * p + ph, :]
                    oview = d_wdo[:, 2 * p + pn, :].rearrange("a (b c) -> a b c", c=2)[:, :, ph]
                wlast[nib] = em.op("vector", lambda v, nib=nib, oview=oview, scl=scl:
                                   v.tensor_tensor(
                    oview, d_mag[:, nib, :].bitcast(BF16),
                    scl, ALU.mult), inc="SDV")
            self.dv_tgt[p] = wlast[3]
            self.prev_dv = wlast[3]
            wr = 0
            if kind in ("g", "u"):
                dst = sc["wg_dec"] if kind == "g" else sc["wu_dec"]
                for nib in range(4):
                    ph, pn = (nib % 2), (nib // 2)
                    hcx = 2 * m + ph
                    em.wait("scalar", "SDV", wlast[nib])
                    dap = bass.AP(dst.tensor, hcx * 128 * NMLP + 512 * q + 128 * pn,
                                  [[NMLP, 128], [256, 2], [1, 128]])
                    wr = em.dma("scalar", dap, d_out[:, 4 * p + nib, :], f"GDWR{p}")
            else:
                em.wait("scalar", "SDV", wlast[3])
                for pn in range(2):
                    r0 = 256 * m + 128 * pn
                    wr = em.dma("scalar", sc["wd_dec"][r0:r0 + 128, 512 * q:512 * q + 512],
                                d_wdo[:, 2 * p + pn, :], f"GDWR{p}")
            self.wr_tgt[p] = wr

    dec = Decode()

    xn_tm = carve(aren, 0, 8192, BF16, [2, 2048])
    xn_sb = carve(aren, 16384, 8192, BF16, [2, HC, 128])
    qs_sb = carve(aren, 24576, 8192, BF16, [2, D])

    qk_evac_dv = [0, 0, 0, 0]
    xn_pe = [0, 0]
    xn_wr = [0, 0]
    xntm_pe = [0, 0]
    qs_wr = [0, 0]
    xrow_act = 0
    xrow_dv = 0
    var_par = [0, 0]
    tcopy_act = [0, 0, 0, 0]

    stats_par = [stat_sb, bst_sb]

    for st in range(NSTRIP):
        tok0 = st * 128
        pp = st % 2
        stp = stats_par[pp]
        em.wait("sync", "SACT", xrow_act)
        em.wait("sync", "SDV", xrow_dv)
        ld_r = em.dma("sync", xrow_sb, x_tm[tok0:tok0 + 128, :], "LXR")

        em.wait("vector", "LXR", ld_r)
        rs_dv = em.op("vector", lambda v, stp=stp: v.reduce_sum(
            stp[:, 0:1], xrow_sb, axis=AX.X), inc="SDV")
        em.wait("scalar", "LXR", ld_r)
        em.wait("scalar", "SDV", var_par[pp])
        if st == 0:
            em.wait("scalar", "SDV", ones_dv)
        sq1_act = em.op("scalar", lambda s2, stp=stp: s2.activation(
            out=sqs_sb, in_=xrow_sb, func=AF.Square, accum_out=stp[:, 1:2]), inc="SACT")
        xrow_act = sq1_act
        em.wait("vector", "SDV", rs_dv)
        mean_dv = em.op("vector", lambda v, stp=stp: v.tensor_scalar(
            stp[:, 2:3], stp[:, 0:1], 1.0 / D, None, ALU.mult), inc="SDV")
        em.wait("vector", "SDV", mean_dv)
        m2_dv = em.op("vector", lambda v, stp=stp: v.tensor_scalar(
            stp[:, 3:4], stp[:, 2:3], stp[:, 2:3], None, ALU.mult), inc="SDV")
        em.wait("vector", "SDV", m2_dv)
        em.wait("vector", "SACT", sq1_act)
        var_dv = em.op("vector", lambda v, stp=stp: v.scalar_tensor_tensor(
            stp[:, 4:5], stp[:, 1:2], 1.0 / D, stp[:, 3:4],
            ALU.mult, ALU.subtract), inc="SDV")
        var_par[pp] = var_dv
        em.wait("scalar", "SDV", var_dv)
        sd_act = em.op("scalar", lambda s2, stp=stp: s2.activation(
            out=stp[:, 5:6], in_=stp[:, 4:5], func=AF.Sqrt,
            bias=eps_sb[:, 0:1]), inc="SACT")
        em.wait("vector", "SACT", sd_act)
        rc_dv = em.op("vector", lambda v, stp=stp: v.reciprocal(
            stp[:, 3:4], stp[:, 5:6]), inc="SDV")

        em.wait("vector", "SDV", rc_dv)
        em.wait("vector", "SPE", xntm_pe[pp])
        in4 = xrow_sb.rearrange("p (b i q) -> p b q i", b=8, i=128, q=2)
        out4 = xn_tm[:, pp].rearrange("p (b q i) -> p b q i", b=8, q=2, i=128)
        xntm_dv = em.op("vector", lambda v, in4=in4, out4=out4, stp=stp: v.tensor_scalar(
            out4, in4, stp[:, 2:3], stp[:, 3:4], ALU.subtract, ALU.mult), inc="SDV")
        xrow_dv = xntm_dv

        dec.emit(7)
        em.wait("tensor", "SDV", xntm_dv)
        if st == 0:
            em.wait("tensor", "GCAST", 32)
        cp_act = 0
        for hc in range(HC):
            bk = hc % 4
            tp_b = pb[bk].ap().bitcast(BF16)[:, 0:128]
            em.wait("tensor", "SACT", tcopy_act[bk])
            tpe = em.op("tensor", lambda tn, hc=hc, pp=pp, tp_b=tp_b: tn.transpose(
                tp_b, xn_tm[:, pp, 128 * hc:128 * hc + 128], idb_sb), inc="SPE")
            if hc == HC - 1:
                xntm_pe[pp] = tpe
            em.wait("scalar", "SPE", tpe)
            if hc == 0:
                em.wait("scalar", "SPE", xn_pe[pp])
                em.wait("scalar", f"GXW{pp}", xn_wr[pp])
            cp_act = em.op("scalar", lambda s2, hc=hc, pp=pp, tp_b=tp_b: s2.copy(
                xn_sb[:, pp, hc, :], tp_b), inc="SACT")
            tcopy_act[bk] = cp_act
        xn_ready = cp_act

        if st < 32:
            em.wait("gpsimd", "SACT", xn_ready)
            dstap = bass.AP(sc["xn_sc"].tensor, tok0, [[TLOC, 128], [128 * TLOC, HC], [1, 128]])
            xn_wr[pp] = em.dma("gpsimd", dstap, xn_sb[:, pp], f"GXW{pp}")

        em.wait("tensor", "SACT", xn_ready)
        if st == 0:
            em.wait("tensor", "GCAST", wqk_ld)
        for kq in range(2):
            wsb = wq_sb if kq == 0 else wk_sb
            mm = None
            for hc in range(HC):
                for e5 in range(4):
                    if hc == 0:
                        em.wait("tensor", "SDV", qk_evac_dv[e5])
                    last = (hc == HC - 1 and e5 == 3)
                    mm = em.op("tensor", lambda tn, wsb=wsb, hc=hc, e5=e5, pp=pp: tn.matmul(
                        pb[4 + e5][:, :], xn_sb[:, pp, hc, :],
                        wsb[:, hc, 512 * e5:512 * e5 + 512],
                        start=(hc == 0), stop=(hc == HC - 1)),
                        inc="SPE" if last else None)
            pe_qk = mm
            if kq == 1:
                xn_pe[pp] = pe_qk
            em.wait("vector", "SPE", pe_qk)
            em.wait("vector", f"GQW{kq}", qs_wr[kq])
            for e5 in range(4):
                qk_evac_dv[e5] = em.op("vector", lambda v, e5=e5, kq=kq: v.tensor_copy(
                    qs_sb[:, kq, 512 * e5:512 * e5 + 512], pb[4 + e5][:, :]), inc="SDV")
            em.wait("gpsimd", "SDV", qk_evac_dv[3])
            dst = sc["q_sc"] if kq == 0 else sc["k_sc"]
            qs_wr[kq] = em.dma("gpsimd", dst[tok0:tok0 + 128, :], qs_sb[:, kq], f"GQW{kq}")

    a_pe = em.now("SPE")
    dec_wr_all = (em.now("GDWR0"), em.now("GDWR1"))
    if last_phase == 'A':
        return _finish(nc, em, dbg, dbg_outs, sc)

    q_sb = carve(big0, 0, 65536, BF16, [64, 512])
    k_sb = carve(big1, 0, 32768, BF16, [2, 8, 1024])
    sco_sb = carve(big1, 32768, 16384, F32, [2, 2048])
    exp_sb = carve(big1, 49152, 16384, F32, [2, 2048])
    asb = carve(aren, 0, 8192, BF16, [2, 2048])

    blk_ev = [0, 0]
    k_pe = [0, 0]
    a_wr = 0
    bi_g = 0
    es_prev_pe = a_pe

    for es in range(4):
        em.wait("sync", "SPE", es_prev_pe)
        if es == 0:
            em.wait("sync", "GQW0", qs_wr[0])
            em.wait("sync", "GQW1", qs_wr[1])
        ldq = 0
        for g8 in range(8):
            qap = bass.AP(sc["q_sc"].tensor, (8 * g8) * 128 * D + 512 * es,
                          [[D, 128], [D * 128, 8], [1, 512]])
            ldq = em.dma("sync", q_sb[:, 8 * g8:8 * g8 + 8, :], qap, "LQS")
        em.wait("tensor", "LQS", ldq)
        for ep in range(2):
            for fh in range(2):
                seti = bi_g % 2
                bks = [pb[4 * seti + i] for i in range(4)]
                mm = None
                for sc8 in range(8):
                    kp = sc8 % 2
                    em.wait("sync", "SPE", k_pe[kp])
                    kap = bass.AP(sc["k_sc"].tensor, (1024 * sc8) * D + 1024 * fh,
                                  [[D, 128], [D * 128, 8], [1, 1024]])
                    ldk = em.dma("sync", k_sb[:, kp], kap, f"LKC{kp}")
                    em.wait("tensor", f"LKC{kp}", ldk)
                    if sc8 == 0:
                        em.wait("tensor", "SDV", blk_ev[seti])
                    for stt in range(8):
                        for e1 in range(2):
                            for f5 in range(2):
                                last = (stt == 7 and e1 == 1 and f5 == 1)
                                mm = em.op("tensor", lambda tn, stt=stt, e1=e1, f5=f5,
                                           sc8=sc8, ep=ep, kp=kp, bks=bks: tn.matmul(
                                    bks[2 * e1 + f5][:, :],
                                    q_sb[:, 8 * sc8 + stt,
                                         128 * (2 * ep + e1):128 * (2 * ep + e1) + 128],
                                    k_sb[:, kp, stt, 512 * f5:512 * f5 + 512],
                                    start=(sc8 == 0 and stt == 0),
                                    stop=(sc8 == 7 and stt == 7)),
                                    inc="SPE" if last else None)
                    k_pe[kp] = mm
                pe_blk = k_pe[1]
                em.wait("vector", "SPE", pe_blk)
                for e1 in range(2):
                    for f5 in range(2):
                        blk_ev[seti] = em.op("vector", lambda v, e1=e1, f5=f5, fh=fh,
                                             bks=bks: v.tensor_copy(
                            sco_sb[:, e1, 1024 * fh + 512 * f5:1024 * fh + 512 * f5 + 512],
                            bks[2 * e1 + f5][:, :]), inc="SDV")
                bi_g += 1
                dec.emit(8)
            em.wait("vector", "SDV", blk_ev[0])
            em.wait("vector", "SDV", blk_ev[1])
            em.wait("vector", "GAW", a_wr)
            for e1 in range(2):
                mx_dv = em.op("vector", lambda v, e1=e1: v.reduce_max(
                    bst_sb[:, 0:1], sco_sb[:, e1, :], axis=AX.X), inc="SDV")
                em.wait("vector", "SDV", mx_dv)
                sm1_dv = em.op("vector", lambda v: v.tensor_scalar(
                    bst_sb[:, 1:2], bst_sb[:, 0:1], -0.125, None, ALU.mult), inc="SDV")
                em.wait("scalar", "SDV", sm1_dv)
                ex_act = em.op("scalar", lambda s2, e1=e1: s2.activation(
                    out=exp_sb[:, e1, :], in_=sco_sb[:, e1, :], func=AF.Exp,
                    bias=bst_sb[:, 1:2], scale=0.125, accum_out=bst_sb[:, 2:3]), inc="SACT")
                em.wait("vector", "SACT", ex_act)
                rc2_dv = em.op("vector", lambda v: v.reciprocal(
                    bst_sb[:, 3:4], bst_sb[:, 2:3]), inc="SDV")
                em.wait("vector", "SDV", rc2_dv)
                a_dv = em.op("vector", lambda v, e1=e1: v.tensor_scalar(
                    asb[:, e1, :], exp_sb[:, e1, :], bst_sb[:, 3:4], None, ALU.mult), inc="SDV")
                em.wait("gpsimd", "SDV", a_dv)
                erow = 512 * es + 256 * ep + 128 * e1
                a_wr = em.dma("gpsimd", sc["a_sc"][erow:erow + 128, :], asb[:, e1, :], "GAW")
        es_prev_pe = em.now("SPE")

    b_pe = em.now("SPE")
    if last_phase == 'B':
        dec.emit(999)
        return _finish(nc, em, dbg, dbg_outs, sc)

    wv_sb = carve(big0, 0, 65536, BF16, [HC, D])
    em.wait("gpsimd", "SPE", b_pe)
    wv_ld = em.dma("gpsimd", wv_sb, wrows(wv_i), "GCAST")
    at_sb = carve(big1, 0, 65536, BF16, [HC, 2048])
    vtb_sb = carve(aren, 0, 16384, BF16, [HC, 512])
    xnc = carve(aren, 16384, 16384, BF16, [HC, 512])
    arow = ist_sb.rearrange("p a b -> p (a b)")
    c2st = ist_sb

    tb_dv = [0, 0]
    at_ev = 0
    for ec in range(HC):
        if ec == 0:
            em.wait("sync", "GAW", a_wr)
            em.wait("sync", "SPE", b_pe)
        else:
            em.wait("sync", "SDV", at_ev)
        lda = em.dma("sync", arow, sc["a_sc"][128 * ec:128 * ec + 128, :], "LAR")
        em.wait("tensor", "LAR", lda)
        for fc in range(HC):
            tp_b = pb[4 + fc % 2].ap().bitcast(BF16)[:, 0:128]
            em.wait("tensor", "SDV", tb_dv[fc % 2])
            tpe = em.op("tensor", lambda tn, fc=fc, tp_b=tp_b: tn.transpose(
                tp_b, arow[:, 128 * fc:128 * fc + 128], idb_sb), inc="SPE")
            em.wait("vector", "SPE", tpe)
            tb_dv[fc % 2] = em.op("vector", lambda v, fc=fc, ec=ec, tp_b=tp_b: v.tensor_copy(
                at_sb[:, fc, 128 * ec:128 * ec + 128], tp_b), inc="SDV")
        at_ev = tb_dv[(HC - 1) % 2]
    at_done_dv = em.now("SDV")

    vban_dv = [0, 0, 0, 0]
    c2_wr = [0, 0]
    ob_dv = [0, 0, 0, 0]
    xnc_pe = 0
    for sb8 in range(8):
        em.wait("sync", "SPE", xnc_pe if sb8 > 0 else b_pe)
        if sb8 == 0:
            em.wait("sync", "SDV", at_done_dv)
        xap = bass.AP(sc["xn_sc"].tensor, 512 * sb8, [[TLOC, 128], [128 * TLOC, HC], [1, 512]])
        ldxn = em.dma("sync", xnc, xap, "LXC")
        em.wait("tensor", "LXC", ldxn)
        if sb8 == 0:
            em.wait("tensor", "GCAST", wv_ld)
            em.wait("tensor", "SDV", at_done_dv)
        for fc in range(HC):
            bk = fc % 4
            mm = None
            for hc in range(HC):
                if hc == 0:
                    em.wait("tensor", "SDV", vban_dv[bk])
                mm = em.op("tensor", lambda tn, fc=fc, hc=hc, bk=bk: tn.matmul(
                    pb[bk][:, :], wv_sb[:, hc, 128 * fc:128 * fc + 128],
                    xnc[:, hc, :], start=(hc == 0), stop=(hc == HC - 1)),
                    inc="SPE" if hc == HC - 1 else None)
            vpe = mm
            if fc == HC - 1:
                xnc_pe = vpe
            em.wait("vector", "SPE", vpe)
            vban_dv[bk] = em.op("vector", lambda v, fc=fc, bk=bk: v.tensor_copy(
                vtb_sb[:, fc, :], pb[bk][:, :]), inc="SDV")
        vt_dv = em.now("SDV")
        dec.emit(6)
        em.wait("tensor", "SDV", vt_dv)
        for ec in range(HC):
            bk = 4 + ec % 4
            mm = None
            for fc in range(HC):
                if fc == 0:
                    em.wait("tensor", "SDV", ob_dv[ec % 4])
                mm = em.op("tensor", lambda tn, ec=ec, fc=fc, bk=bk: tn.matmul(
                    pb[bk][:, :], at_sb[:, fc, 128 * ec:128 * ec + 128],
                    vtb_sb[:, fc, :], start=(fc == 0), stop=(fc == HC - 1)),
                    inc="SPE" if fc == HC - 1 else None)
            mpe = mm
            em.wait("vector", "SPE", mpe)
            em.wait("vector", f"GC2W{ec % 2}", c2_wr[ec % 2])
            ob_dv[ec % 4] = em.op("vector", lambda v, ec=ec, bk=bk: v.tensor_copy(
                c2st[:, ec % 2, :], pb[bk][:, :]), inc="SDV")
            em.wait("gpsimd", "SDV", ob_dv[ec % 4])
            off = 512 * sb8
            c2_wr[ec % 2] = em.dma("gpsimd", sc["attnT_sc"][ec, :, off:off + 512],
                                   c2st[:, ec % 2, :], f"GC2W{ec % 2}")
        dec.emit(6)

    c2_pe = em.now("SPE")
    c2_wr_all = (em.now("GC2W0"), em.now("GC2W1"))
    if last_phase == 'C':
        return _finish(nc, em, dbg, dbg_outs, sc)

    wo_sb = carve(big0, 0, 65536, BF16, [HC, D])
    em.wait("gpsimd", "SPE", c2_pe)
    wo_ld = em.dma("gpsimd", wo_sb, wrows(wo_i), "GCAST")
    atn_sb = carve(big1, 0, 32768, BF16, [2, HC, 512])
    af_st = carve(aren, 0, 8192, F32, [4, 512])

    atn_pe = [0, 0]
    c3ob_dv = [0, 0, 0, 0, 0, 0, 0, 0]
    af_wr = [0, 0]
    for s5 in range(8):
        ap_ = s5 % 2
        em.wait("sync", "SPE", atn_pe[ap_])
        if s5 == 0:
            em.wait("sync", "GC2W0", c2_wr_all[0])
            em.wait("sync", "GC2W1", c2_wr_all[1])
            em.wait("sync", "SPE", c2_pe)
        aap = bass.AP(sc["attnT_sc"].tensor, 512 * s5,
                      [[TLOC, 128], [128 * TLOC, HC], [1, 512]])
        lda = em.dma("sync", atn_sb[:, ap_], aap, f"LAT{ap_}")
        em.wait("tensor", f"LAT{ap_}", lda)
        if s5 == 0:
            em.wait("tensor", "GCAST", wo_ld)
        dec.emit(10)
        dec.emit(12)
        bset = 4 * (s5 % 2)
        for st1 in range(4):
            mm = None
            for ec in range(HC):
                for g5 in range(4):
                    if ec == 0:
                        em.wait("tensor", "SDV", c3ob_dv[bset + g5])
                    last = (ec == HC - 1 and g5 == 3)
                    mm = em.op("tensor", lambda tn, ap_=ap_, st1=st1, ec=ec, g5=g5,
                               bset=bset: tn.matmul(
                        pb[bset + g5][:, :],
                        atn_sb[:, ap_, ec, 128 * st1:128 * st1 + 128],
                        wo_sb[:, ec, 512 * g5:512 * g5 + 512],
                        start=(ec == 0), stop=(ec == HC - 1)),
                        inc="SPE" if last else None)
            mpe = mm
            atn_pe[ap_] = mpe
            em.wait("vector", "SPE", mpe)
            em.wait("vector", f"GAFW{s5 % 2}", af_wr[s5 % 2])
            for g5 in range(4):
                c3ob_dv[bset + g5] = em.op("vector", lambda v, g5=g5, bset=bset: v.tensor_copy(
                    af_st[:, g5, :], pb[bset + g5][:, :]), inc="SDV")
            em.wait("gpsimd", "SDV", c3ob_dv[bset + 3])
            tok0 = 512 * s5 + 128 * st1
            af_wr[s5 % 2] = em.dma("gpsimd", sc["afin_sc"][tok0:tok0 + 128, :],
                                   af_st.rearrange("p a b -> p (a b)"), f"GAFW{s5 % 2}")

    dec.emit(999)
    c3_pe = em.now("SPE")
    c3_wr_all = (em.now("GAFW0"), em.now("GAFW1"))
    if last_phase == 'C3':
        return _finish(nc, em, dbg, dbg_outs, sc)

    xnh_sb = carve(big0, 0, 65536, BF16, [HC, 2048])
    wgu_sb = carve(big1, 0, 32768, BF16, [2, 2, HC, 256])

    gel_dv = [0, 0, 0, 0]
    sig_war = [0, 0, 0, 0]
    ist_wr = [0, 0, 0, 0]
    wgu_pe = [0, 0]
    d_first = True
    for dh in range(2):
        em.wait("sync", "SPE", c3_pe if dh == 0 else em.now("SPE"))
        ldh = 0
        for hc in range(HC):
            xap = bass.AP(sc["xn_sc"].tensor, hc * 128 * TLOC + 2048 * dh,
                          [[TLOC, 128], [1, 2048]])
            ldh = em.dma("sync", xnh_sb[:, hc, :], xap, "LXH")
        for ntp in range(64):
            wp = ntp % 2
            em.wait("sync", "SPE", wgu_pe[wp])
            if d_first:
                em.wait("sync", "GDWR0", dec_wr_all[0])
                em.wait("sync", "GDWR1", dec_wr_all[1])
                d_first = False
            gap = bass.AP(sc["wg_dec"].tensor, 256 * ntp, [[NMLP, 128], [128 * NMLP, HC], [1, 256]])
            uap = bass.AP(sc["wu_dec"].tensor, 256 * ntp, [[NMLP, 128], [128 * NMLP, HC], [1, 256]])
            em.dma("sync", wgu_sb[:, 0, wp], gap, f"LWG{wp}")
            ldw = em.dma("sync", wgu_sb[:, 1, wp], uap, f"LWG{wp}")
            for nt1 in range(2):
                nt = 2 * ntp + nt1
                em.wait("tensor", f"LWG{wp}", ldw)
                if ntp == 0 and nt1 == 0:
                    em.wait("tensor", "LXH", ldh)
                gpe = upe = None
                for gu in range(2):
                    mm = None
                    for hc in range(HC):
                        for t5 in range(4):
                            if hc == 0:
                                em.wait("tensor", "SDV", gel_dv[t5])
                            last = (hc == HC - 1 and t5 == 3)
                            mm = em.op("tensor", lambda tn, gu=gu, hc=hc, t5=t5, wp=wp,
                                       nt1=nt1: tn.matmul(
                                pb[4 * gu + t5][:, :],
                                wgu_sb[:, gu, wp, hc, 128 * nt1:128 * nt1 + 128],
                                xnh_sb[:, hc, 512 * t5:512 * t5 + 512],
                                start=(hc == 0), stop=(hc == HC - 1)),
                                inc="SPE" if last else None)
                    if gu == 0:
                        gpe = mm
                    else:
                        upe = mm
                wgu_pe[wp] = upe
                for t5 in range(4):
                    gb, ub = pb[t5], pb[4 + t5]
                    gslot = gel_sb[:, t5, :]
                    qslot = sqs_sb[:, 512 * t5:512 * t5 + 512]
                    em.wait("scalar", "SPE", gpe)
                    em.wait("scalar", "SDV", gel_dv[t5])
                    s2m = em.op("scalar", lambda s2_, gb=gb, gslot=gslot: s2_.activation(
                        out=gslot, in_=gb[:, :], func=AF.Square), inc="SACT")
                    em.wait("vector", "SACT", s2m)
                    em.wait("vector", f"GIW{t5}", ist_wr[t5])
                    uum = em.op("vector", lambda v, gslot=gslot: v.tensor_scalar(
                        gslot, gslot, GELU_B, 1.0, ALU.mult, ALU.add), inc="SDV")
                    em.wait("vector", "SDV", uum)
                    w1m = em.op("vector", lambda v, gb=gb, gslot=gslot: v.tensor_tensor(
                        gslot, gslot, gb[:, :], ALU.mult), inc="SDV")
                    em.wait("scalar", "SDV", w1m)
                    em.wait("scalar", "SDV", sig_war[t5])
                    sgm = em.op("scalar", lambda s2, gslot=gslot, qslot=qslot: s2.activation(
                        out=qslot, in_=gslot, func=AF.Sigmoid, scale=GELU_A), inc="SACT")
                    em.wait("vector", "SACT", sgm)
                    w2m = em.op("vector", lambda v, gb=gb, gslot=gslot, qslot=qslot:
                                v.tensor_tensor(gslot, qslot, gb[:, :], ALU.mult), inc="SDV")
                    sig_war[t5] = w2m
                    em.wait("vector", "SDV", w2m)
                    em.wait("vector", "SPE", upe)
                    gel_dv[t5] = em.op("vector", lambda v, ub=ub, t5=t5, gslot=gslot:
                                       v.tensor_tensor(
                        ist_sb[:, t5, :], gslot, ub[:, :], ALU.mult), inc="SDV")
                    em.wait("gpsimd", "SDV", gel_dv[t5])
                    off = 2048 * dh + 512 * t5
                    ist_wr[t5] = em.dma("gpsimd", sc["inter_sc"][nt, :, off:off + 512],
                                        ist_sb[:, t5, :], f"GIW{t5}")

    d_pe = em.now("SPE")
    d_wr_all = tuple(em.now(f"GIW{i}") for i in range(4))
    if last_phase == 'D':
        return _finish(nc, em, dbg, dbg_outs, sc)

    wdt_lo = carve(big0, 0, 65536, BF16, [64, 512])
    wdt_hi = carve(big1, 0, 65536, BF16, [64, 512])

    def wdt(nt):
        return wdt_lo[:, nt, :] if nt < 64 else wdt_hi[:, nt - 64, :]

    ich_sb = carve(aren, 0, 32768, BF16, [2, HC, 512])
    xs_sb = xrow_sb.rearrange("p (a b) -> p a b", a=4)
    as_sb = sqs_sb.rearrange("p (a b) -> p a b", a=4)

    e_first = True
    ich_pe = [0, 0]
    eob_dv = [0, 0, 0, 0, 0, 0, 0, 0]
    o_wr = [0, 0]
    xa_dv = 0
    wdt_pe = 0
    for hp in range(4):
        em.wait("sync", "SPE", wdt_pe)
        if e_first:
            em.wait("sync", "SPE", d_pe)
            for i4 in range(4):
                em.wait("sync", f"GIW{i4}", d_wr_all[i4])
            e_first = False
        ldwd = 0
        for i in range(16):
            wap = bass.AP(sc["wd_dec"].tensor, (1024 * i) * D + 512 * hp,
                          [[D, 128], [D * 128, 8], [1, 512]])
            ldwd = em.dma("sync",
                          (wdt_lo[:, 8 * i:8 * i + 8, :] if i < 8
                           else wdt_hi[:, 8 * (i - 8):8 * (i - 8) + 8, :]),
                          wap, "LWD")
        for t5 in range(8):
            bset = 4 * (t5 % 2)
            em.wait("sync", "SDV", xa_dv)
            if hp == 0 and t5 == 0:
                em.wait("sync", "GAFW0", c3_wr_all[0])
                em.wait("sync", "GAFW1", c3_wr_all[1])
            xap = bass.AP(x_tm.tensor, (512 * t5) * D + 512 * hp,
                          [[D, 128], [D * 128, 4], [1, 512]])
            aap = bass.AP(sc["afin_sc"].tensor, (512 * t5) * D + 512 * hp,
                          [[D, 128], [D * 128, 4], [1, 512]])
            em.dma("sync", xs_sb, xap, "LXA")
            ldxa = em.dma("sync", as_sb, aap, "LXA")
            mm = None
            for e8 in range(8):
                ip = e8 % 2
                em.wait("sync", "SPE", ich_pe[ip])
                iap = bass.AP(sc["inter_sc"].tensor, (16 * e8) * 128 * TLOC + 512 * t5,
                              [[TLOC, 128], [128 * TLOC, HC], [1, 512]])
                ldi = em.dma("sync", ich_sb[:, ip], iap, f"LIC{ip}")
                em.wait("tensor", f"LIC{ip}", ldi)
                if e8 == 0:
                    em.wait("tensor", "LWD", ldwd)
                    for t1 in range(4):
                        em.wait("tensor", "SDV", eob_dv[bset + t1])
                for ntl in range(HC):
                    nt = 16 * e8 + ntl
                    for t1 in range(4):
                        last = (ntl == HC - 1 and t1 == 3)
                        mm = em.op("tensor", lambda tn, ip=ip, ntl=ntl, nt=nt, t1=t1,
                                   bset=bset: tn.matmul(
                            pb[bset + t1][:, :],
                            ich_sb[:, ip, ntl, 128 * t1:128 * t1 + 128],
                            wdt(nt), start=(nt == 0), stop=(nt == 127)),
                            inc="SPE" if last else None)
                ich_pe[ip] = mm
            mpe = ich_pe[1]
            if t5 == 7:
                wdt_pe = mpe
            em.wait("vector", "SPE", mpe)
            em.wait("vector", "LXA", ldxa)
            for t1 in range(4):
                osl = 2 + t1 % 2
                em.wait("vector", f"GOW{t1 % 2}", o_wr[t1 % 2])
                r1 = em.op("vector", lambda v, t1=t1, bset=bset: v.tensor_tensor(
                    gel_sb[:, 0, :], as_sb[:, t1, :], pb[bset + t1][:, :], ALU.add), inc="SDV")
                em.wait("vector", "SDV", r1)
                eob_dv[bset + t1] = em.op("vector", lambda v, t1=t1, osl=osl: v.tensor_tensor(
                    gel_sb[:, osl, :], gel_sb[:, 0, :], xs_sb[:, t1, :], ALU.add), inc="SDV")
                em.wait("gpsimd", "SDV", eob_dv[bset + t1])
                tok0 = 512 * t5 + 128 * t1
                o_wr[t1 % 2] = em.dma("gpsimd",
                                      out_loc[tok0:tok0 + 128, 512 * hp:512 * hp + 512],
                                      gel_sb[:, osl, :], f"GOW{t1 % 2}")
            xa_dv = eob_dv[bset + 3]

    return _finish(nc, em, dbg, dbg_outs, sc)


def _finish(nc, em, dbg, dbg_outs, sc):
    for name in dbg:
        em.dma("gpsimd", dbg_outs[name], sc[name], "GDBG")
    for s_ in em.dma_sems:
        eng = "gpsimd" if s_.startswith("G") else "sync"
        em.wait(eng, s_, em.now(s_))
    with nc.Block() as block:
        @block.sync
        def _(e):
            for f in em.streams["sync"]:
                f(e)

        @block.gpsimd
        def _(e):
            for f in em.streams["gpsimd"]:
                f(e)

        @block.scalar
        def _(e):
            for f in em.streams["scalar"]:
                f(e)

        @block.vector
        def _(e):
            for f in em.streams["vector"]:
                f(e)

        @block.tensor
        def _(e):
            for f in em.streams["tensor"]:
                f(e)
    counts = {k: len(v) for k, v in em.streams.items()}
    return nc, counts


def host_inputs(inp):
    import ml_dtypes
    perm = _perm_h()
    x = np.asarray(inp["L0_attn_x"], dtype=np.float32)
    wq = np.asarray(inp["L0_attn_Wq"], dtype=np.float32)
    wk = np.asarray(inp["L0_attn_Wk"], dtype=np.float32)
    wv = np.asarray(inp["L0_attn_Wv"], dtype=np.float32)
    wo = np.asarray(inp["L0_attn_Wo"], dtype=np.float32)

    def pack_t(p_int32):
        b = np.ascontiguousarray(p_int32).view(np.uint8).reshape(
            p_int32.shape[0], p_int32.shape[1], 4)[:, :, 0]
        bt = np.ascontiguousarray(b.T)
        return bt.view("<i2")

    pg = pack_t(np.asarray(inp["L0_ffn_gate_W_packed"]))
    pu = pack_t(np.asarray(inp["L0_ffn_up_W_packed"]))
    pd = pack_t(np.asarray(inp["L0_ffn_down_W_packed"]))
    sgt = np.ascontiguousarray(np.asarray(inp["L0_ffn_gate_scales"], dtype=np.float32).T)
    sut = np.ascontiguousarray(np.asarray(inp["L0_ffn_up_scales"], dtype=np.float32).T)
    sdt = np.ascontiguousarray(np.asarray(inp["L0_ffn_down_scales"], dtype=np.float32).T)
    sg0 = np.ascontiguousarray(sgt[:, 0::2])
    sg1 = np.ascontiguousarray(sgt[:, 1::2])
    su0 = np.ascontiguousarray(sut[:, 0::2])
    su1 = np.ascontiguousarray(sut[:, 1::2])
    sd0 = np.ascontiguousarray(sdt[:, 0::2])
    sd1 = np.ascontiguousarray(sdt[:, 1::2])
    shared = {
        "wq": np.ascontiguousarray(wq[perm]),
        "wk": np.ascontiguousarray(wk[perm]),
        "wv": np.ascontiguousarray(wv[perm]),
        "wo": np.ascontiguousarray(wo),
        "idb": np.eye(128, dtype=np.float32).astype(ml_dtypes.bfloat16),
    }
    maps = []
    for c in range(8):
        b, h = c // 2, c % 2
        lf = np.concatenate([np.arange(h * 4096, (h + 1) * 4096),
                             np.arange((1 - h) * 4096, (2 - h) * 4096)])
        xb = x[b][lf]
        m = dict(shared)
        m["x_tm"] = np.ascontiguousarray(xb)
        m["pg16"] = pg
        m["pu16"] = pu
        m["pd16"] = pd
        m["sg0"] = sg0
        m["sg1"] = sg1
        m["su0"] = su0
        m["su1"] = su1
        m["sd0"] = sd0
        m["sd1"] = sd1
        maps.append(m)
    return maps


def kernel(**inputs):
    nc, _ = build_program(dbg=DBG)
    maps = host_inputs(inputs)
    res = run_bass_kernel_spmd(nc, maps, core_ids=list(range(8)))
    out = np.empty((B_, S, D), dtype=np.float32)
    for c in range(8):
        b, h = c // 2, c % 2
        out[b, h * 4096:(h + 1) * 4096, :] = res.results[c]["out_loc"]
    return out


if __name__ == "__main__":
    nc, counts = build_program()
    print("instruction closures per engine:", counts)
